# version 14
# baseline (speedup 1.0000x reference)
import math
from contextlib import ExitStack

import numpy as np

import concourse.bass as bass
import concourse.mybir as mybir
from concourse.bass_utils import run_bass_kernel_spmd

F32 = mybir.dt.float32
BF16 = mybir.dt.bfloat16
AT = mybir.ActivationFunctionType
OP = mybir.AluOpType
AX = mybir.AxisListType.X

S = 4096
DM = 2048
HD = 128
NHC = 4
FF = 4096
TCH = 1024
NEG = -1.0e9
NKT = S // 128
NQT = S // 128
BIS_ITERS = 40
EPS = 1e-5

_DUMP = ()


class Prog:
    ENGS = ("t", "v", "s", "g", "y")

    def __init__(self):
        self.ops = {e: [] for e in self.ENGS}
        self.sem_names = []
        self.counts = {}

    def sem(self, name):
        self.sem_names.append(name)
        self.counts[name] = 0
        return name

    def do(self, eng, fn, inc=None, inc_n=1):
        if inc is not None:
            self.counts[inc] += inc_n
        self.ops[eng].append(("do", fn, inc, inc_n))

    def wait(self, eng, sem, val=None):
        v = self.counts[sem] if val is None else val
        self.ops[eng].append(("wait", sem, v))

    def val(self, sem):
        return self.counts[sem]

    def barrier(self, engs=None):
        engs = engs or self.ENGS
        for e in engs:
            self.do(e, lambda eng, sems: eng.nop(), inc="bar")
        tot = self.counts["bar"]
        for e in engs:
            self.wait(e, "bar", tot)

    def replay(self, eng_name, eng, sems):
        for item in self.ops[eng_name]:
            if item[0] == "do":
                _, fn, inc, inc_n = item
                instr = fn(eng, sems)
                if inc is not None:
                    assert instr is not None
                    instr.then_inc(sems[inc], inc_n)
            else:
                _, sem, v = item
                if v > 0:
                    eng.wait_ge(sems[sem], v)


def _host_constants():
    c = {}
    c["ident"] = np.eye(128, dtype=np.float32)
    rp = np.zeros((128, 128), dtype=np.float32)
    for pp in range(128):
        rp[(pp + 64) % 128, pp] = 1.0
    c["rotperm"] = rp
    i = np.arange(128)
    c["tri_add_qk"] = np.where(i[None, :] <= i[:, None], 0.0, NEG).astype(np.float32)
    c["tri01_kq"] = (i[:, None] <= i[None, :]).astype(np.float32)
    for D in (384, 512):
        c["band01_%d" % D] = ((D + i[None, :] - i[:, None]) <= 409).astype(np.float32)
    inv = (1.0 / (10000.0 ** (np.arange(0, HD, 2, dtype=np.float32) / HD))).astype(np.float32)
    pos = np.arange(S, dtype=np.float32)
    fr = (pos[:, None] * inv[None, :]).astype(np.float32)
    cos = np.cos(fr).astype(np.float32)
    sin = np.sin(fr).astype(np.float32)
    cos_t = np.ascontiguousarray(np.concatenate([cos, cos], axis=1).T)
    sin_full = np.ascontiguousarray(np.concatenate([sin, sin], axis=1).T)
    sgn = np.where(np.arange(128) < 64, -1.0, 1.0).astype(np.float32)[:, None]
    sin_star = (sin_full * sgn).astype(np.float32)
    sc = np.float32(1.0 / math.sqrt(HD))
    c["cos_k"] = cos_t
    c["sin_k"] = sin_star
    c["cos_q"] = (cos_t * sc).astype(np.float32)
    c["sin_q"] = (sin_star * sc).astype(np.float32)
    return c


def build_program(dump=()):
    nc = bass.Bass("TRN2", target_bir_lowering=False, debug=False)
    p = Prog()

    x_in = nc.dram_tensor("x_b", [S, DM], F32, kind="ExternalInput").ap()
    xch_in = nc.dram_tensor("x_chunk", [TCH, DM], F32, kind="ExternalInput").ap()
    wq_in = nc.dram_tensor("wq_t", [DM, 512], F32, kind="ExternalInput").ap()
    wk_in = nc.dram_tensor("wk_t", [DM, 512], F32, kind="ExternalInput").ap()
    wv_in = nc.dram_tensor("wv_t", [DM, 512], F32, kind="ExternalInput").ap()
    wo_in = nc.dram_tensor("wo_tl", [1024, 4096], F32, kind="ExternalInput").ap()
    w1_in = nc.dram_tensor("w1_tl", [FF, DM], F32, kind="ExternalInput").ap()
    w2_in = nc.dram_tensor("w2_tl", [FF, FF], F32, kind="ExternalInput").ap()
    w3_in = nc.dram_tensor("w3_tl", [1024, 8192], F32, kind="ExternalInput").ap()
    b1_in = nc.dram_tensor("b1s", [128, 32], F32, kind="ExternalInput").ap()
    b2_in = nc.dram_tensor("b2s", [128, 32], F32, kind="ExternalInput").ap()
    b3_in = nc.dram_tensor("b3r", [1, DM], F32, kind="ExternalInput").ap()
    sel_in = nc.dram_tensor("selmask", [128, 4], F32, kind="ExternalInput").ap()
    cst = {}
    for name, shape in [("ident", [128, 128]), ("rotperm", [128, 128]),
                        ("tri_add_qk", [128, 128]), ("tri01_kq", [128, 128]),
                        ("band01_384", [128, 128]), ("band01_512", [128, 128]),
                        ("cos_q", [128, S]), ("sin_q", [128, S]),
                        ("cos_k", [128, S]), ("sin_k", [128, S])]:
        cst[name] = nc.dram_tensor("c_" + name, shape, F32, kind="ExternalInput").ap()

    out_c = nc.dram_tensor("out_c", [TCH, DM], F32, kind="ExternalOutput").ap()

    xn_scr = nc.dram_tensor("xn_scr", [S, DM], BF16, kind="Internal")
    xnch_scr = nc.dram_tensor("xnch_scr", [TCH, DM], BF16, kind="Internal")
    cc_in = nc.dram_tensor("cc_in", [512, S], BF16, kind="Internal")
    cc_out = nc.dram_tensor("cc_out", [DM, S], BF16, kind="Internal")

    dump = set(dump)
    dmp = {}
    def _dump_decl(name, shape, dtype=F32):
        if name in dump:
            dmp[name] = nc.dram_tensor("dmp_" + name, shape, dtype, kind="ExternalOutput").ap()
    _dump_decl("xn", [S, DM], BF16)
    _dump_decl("q", [128, NHC * S], BF16)
    _dump_decl("k", [128, NHC * S], BF16)
    _dump_decl("vt", [128, NHC * S], BF16)
    _dump_decl("B4", [128, 8])
    _dump_decl("cs", [128, 128])
    _dump_decl("thr", [128, 8])
    _dump_decl("heavy", [128, 128])
    _dump_decl("hb", [128, 128])
    _dump_decl("attn", [128, NHC * S], BF16)
    _dump_decl("h1", [128, 32 * 512], BF16)
    _dump_decl("h1nb", [128, 32 * 512], BF16)
    _dump_decl("h2", [128, 64 * 512], BF16)
    _dump_decl("h2end", [128, 64 * 512], BF16)
    _dump_decl("agend", [128, 32 * 512], BF16)
    _dump_decl("agat", [128, 32 * 512], BF16)
    _dump_decl("xnf", [128, 16 * 512], BF16)

    for s_ in ["bar", "dx", "dxw", "dxr", "dtab", "dw", "dwf", "dcc", "dout", "dag", "dxf",
               "c_t", "c_v", "c_s", "c_g", "cc_done"]:
        p.sem(s_)

    st_perm = ExitStack()
    st_qkv = ExitStack()
    st_p1 = ExitStack()
    st_p2 = ExitStack()
    st_p3 = ExitStack()
    st_p5 = ExitStack()
    st_p6a = ExitStack()
    st_p6b = ExitStack()

    sb = {}
    def alloc(stack, name, shape, dtype):
        sb[name] = stack.enter_context(nc.sbuf_tensor(name, shape, dtype))
        return sb[name]

    alloc(st_perm, "ident_f", [128, 128], F32)
    alloc(st_perm, "ident_b", [128, 128], BF16)
    alloc(st_perm, "rotperm_b", [128, 128], BF16)
    alloc(st_perm, "tri_add", [128, 128], F32)
    alloc(st_perm, "tri01_b", [128, 128], BF16)
    alloc(st_perm, "band384_b", [128, 128], BF16)
    alloc(st_perm, "band512_b", [128, 128], BF16)
    alloc(st_perm, "ones_b", [128, 128], BF16)
    alloc(st_perm, "ones_f", [128, 128], F32)
    alloc(st_perm, "selm", [128, 4], F32)
    alloc(st_perm, "b1_sb", [128, 32], F32)
    alloc(st_perm, "b2_sb", [128, 32], F32)
    alloc(st_perm, "cs_bias", [128, 128], F32)
    alloc(st_perm, "heavy01", [128, 128], F32)
    alloc(st_perm, "hbias", [128, 128], F32)
    alloc(st_perm, "negB4", [128, 4], F32)
    alloc(st_perm, "scr2", [128, 128], F32)
    alloc(st_perm, "smax", [128, 16], F32)
    alloc(st_perm, "binv", [128, 16], F32)
    alloc(st_perm, "thrb", [128, 4], F32)
    alloc(st_perm, "lnst", [128, 8], F32)

    alloc(st_qkv, "q_all", [128, NHC * S], BF16)
    alloc(st_qkv, "k_all", [128, NHC * S], BF16)
    alloc(st_qkv, "vt_all", [128, NHC * S], BF16)

    alloc(st_p1, "xbuf0", [128, DM], F32)
    alloc(st_p1, "xbuf1", [128, DM], F32)
    alloc(st_p1, "sqdump", [128, DM], BF16)
    alloc(st_p1, "xnout0", [128, DM], BF16)
    alloc(st_p1, "xnout1", [128, DM], BF16)

    def dma(eng_key, sem, dst, src, **kw):
        def f(eng, sems, dst=dst, src=src, kw=kw):
            return eng.dma_start(out=dst, in_=src, **kw)
        p.do(eng_key, f, inc=sem, inc_n=16)

    def vdrain():
        p.do("v", lambda eng, sems: eng.drain())

    def memset(eng_key, dst, v, inc=None):
        def f(eng, sems, dst=dst, v=v):
            return eng.memset(dst, v)
        p.do(eng_key, f, inc=inc)

    dma("g", "dw", sb["ident_b"][:, :], cst["ident"])
    dma("g", "dw", sb["rotperm_b"][:, :], cst["rotperm"])
    dma("g", "dw", sb["tri01_b"][:, :], cst["tri01_kq"])
    dma("g", "dw", sb["band384_b"][:, :], cst["band01_384"])
    dma("g", "dw", sb["band512_b"][:, :], cst["band01_512"])
    dma("y", "dw", sb["ident_f"][:, :], cst["ident"])
    dma("y", "dw", sb["tri_add"][:, :], cst["tri_add_qk"])
    dma("y", "dw", sb["selm"][:, :], sel_in)
    dma("y", "dw", sb["b1_sb"][:, :], b1_in)
    dma("y", "dw", sb["b2_sb"][:, :], b2_in)
    memset("g", sb["ones_b"][:, :], 1.0)
    memset("g", sb["ones_f"][:, :], 1.0)
    memset("g", sb["scr2"][:, :], 0.0)
    memset("v", sb["cs_bias"][:, :], 0.0)
    p.barrier()

    def ln_tiles(src_ap, n_tiles, dst_scr, wsem, sq_buf_name, xn_bufs, xbufs):
        last_apply_cv = [0, 0]
        for it in range(n_tiles):
            xb = xbufs[it % 2]
            xnob = xn_bufs[it % 2]
            if it >= 2:
                p.wait("y", wsem, 16 * (it - 1))
            dma("y", "dx", xb[:, :], src_ap[128 * it:128 * (it + 1), :])
            p.wait("v", "dx")
            p.wait("s", "dx")
            if it >= 1:
                p.wait("s", "c_v", last_apply_cv[0])
            ln = sb["lnst"]
            p.do("v", lambda eng, sems, xb=xb, ln=ln:
                 eng.reduce_sum(ln[:, 0:1], xb[:, :], axis=AX), inc="c_v")
            p.do("s", lambda eng, sems, xb=xb, ln=ln:
                 eng.activation(out=sb[sq_buf_name][:, 0:DM], in_=xb[:, :], func=AT.Square,
                                accum_out=ln[:, 1:2]), inc="c_s")
            p.wait("v", "c_s")
            vdrain()
            p.do("v", lambda eng, sems, ln=ln:
                 eng.tensor_scalar(out=ln[:, 2:3], in0=ln[:, 0:1], scalar1=1.0 / DM,
                                   scalar2=None, op0=OP.mult))
            p.do("v", lambda eng, sems, ln=ln:
                 eng.tensor_scalar(out=ln[:, 3:4], in0=ln[:, 1:2], scalar1=1.0 / DM,
                                   scalar2=None, op0=OP.mult))
            p.do("v", lambda eng, sems, ln=ln:
                 eng.tensor_tensor(out=ln[:, 4:5], in0=ln[:, 2:3], in1=ln[:, 2:3], op=OP.mult))
            vdrain()
            p.do("v", lambda eng, sems, ln=ln:
                 eng.tensor_tensor(out=ln[:, 5:6], in0=ln[:, 3:4], in1=ln[:, 4:5], op=OP.subtract))
            vdrain()
            p.do("v", lambda eng, sems, ln=ln:
                 eng.tensor_scalar(out=ln[:, 5:6], in0=ln[:, 5:6], scalar1=float(EPS),
                                   scalar2=None, op0=OP.add))
            vdrain()
            p.do("v", lambda eng, sems, ln=ln:
                 eng.reciprocal(out=ln[:, 6:7], in_=ln[:, 5:6]), inc="c_v")
            p.wait("s", "c_v")
            p.do("s", lambda eng, sems, ln=ln:
                 eng.activation(out=ln[:, 6:7], in_=ln[:, 6:7], func=AT.Sqrt), inc="c_s")
            p.wait("v", "c_s")
            p.do("v", lambda eng, sems, ln=ln:
                 eng.tensor_tensor(out=ln[:, 7:8], in0=ln[:, 2:3], in1=ln[:, 6:7], op=OP.mult))
            vdrain()
            p.do("v", lambda eng, sems, ln=ln:
                 eng.tensor_scalar(out=ln[:, 7:8], in0=ln[:, 7:8], scalar1=-1.0,
                                   scalar2=None, op0=OP.mult))
            if it >= 2:
                p.wait("v", wsem, 16 * (it - 1))
            vdrain()
            p.do("v", lambda eng, sems, xb=xb, ln=ln, xnob=xnob:
                 eng.tensor_scalar(out=xnob[:, :], in0=xb[:, :],
                                   scalar1=ln[:, 6:7], scalar2=ln[:, 7:8],
                                   op0=OP.mult, op1=OP.add), inc="c_v")
            last_apply_cv[0] = p.val("c_v")
            p.wait("y", "c_v")
            dma("y", wsem, dst_scr.ap()[128 * it:128 * (it + 1), :], xnob[:, :])

    ln_tiles(x_in, S // 128, xn_scr, "dxw", "sqdump", (sb["xnout0"], sb["xnout1"]),
             (sb["xbuf0"], sb["xbuf1"]))
    if "xn" in dmp:
        p.wait("y", "dxw")
        dma("y", "dout", dmp["xn"], xn_scr.ap()[:, :])
    p.barrier()

    st_p1.close()
    alloc(st_p2, "wq_sb", [128, 16 * 512], BF16)
    alloc(st_p2, "wk_sb", [128, 16 * 512], BF16)
    alloc(st_p2, "wv_sb", [128, 16 * 512], BF16)
    alloc(st_p2, "xnp0", [128, 16 * 512], BF16)
    alloc(st_p2, "xnp1", [128, 16 * 512], BF16)
    alloc(st_p2, "tabs", [128, 4 * 512], F32)
    alloc(st_p2, "qpre", [128, 512], BF16)
    alloc(st_p2, "rsc1", [128, 512], F32)
    alloc(st_p2, "rsc2", [128, 512], F32)
    st_psum2 = ExitStack()
    for nm, wsrc in (("wq_sb", wq_in), ("wk_sb", wk_in), ("wv_sb", wv_in)):
        for fc in range(16):
            dma("g", "dw", sb[nm][:, 512 * fc:512 * (fc + 1)],
                wsrc[128 * fc:128 * (fc + 1), :])
    pq0 = st_psum2.enter_context(nc.psum_tensor("pq0", [128, 512], F32))
    pq1 = st_psum2.enter_context(nc.psum_tensor("pq1", [128, 512], F32))
    prot = st_psum2.enter_context(nc.psum_tensor("prot", [128, 512], F32))
    ptv = st_psum2.enter_context(nc.psum_tensor("ptv", [128, 1024], BF16))

    bank_free = {0: 0, 1: 0}
    panel_done_cv = {}
    tab_done_cv = {}
    ptv_free = [0]
    for pan in range(8):
        xnp = (sb["xnp0"], sb["xnp1"])[pan % 2]
        tok0 = 512 * pan
        if pan >= 2:
            p.wait("y", "c_v", panel_done_cv[pan - 2])
        for fc in range(16):
            dma("y", "dxr", xnp[:, 512 * fc:512 * (fc + 1)],
                xn_scr.ap()[tok0:tok0 + 512, 128 * fc:128 * (fc + 1)], transpose=True)
        if pan >= 1:
            p.wait("y", "c_v", tab_done_cv[pan - 1])
        for ti, nm in enumerate(("cos_q", "sin_q", "cos_k", "sin_k")):
            dma("y", "dtab", sb["tabs"][:, 512 * ti:512 * (ti + 1)],
                cst[nm][:, tok0:tok0 + 512])
        p.wait("t", "dxr")
        p.wait("t", "dw")
        for of in range(12):
            kind = "q" if of < 4 else ("k" if of < 8 else "v")
            h = of % 4
            wname = {"q": "wq_sb", "k": "wk_sb", "v": "wv_sb"}[kind]
            pq = (pq0, pq1)[of % 2]
            p.wait("t", "c_v", bank_free[of % 2])
            for fc in range(16):
                def mm(eng, sems, pq=pq, wname=wname, fc=fc, h=h, xnp=xnp,
                       first=(fc == 0), last=(fc == 15)):
                    return eng.matmul(pq[:, :],
                                      sb[wname][:, 512 * fc + 128 * h:512 * fc + 128 * (h + 1)],
                                      xnp[:, 512 * fc:512 * (fc + 1)],
                                      start=first, stop=last, skip_group_check=True)
                p.do("t", mm, inc=("c_t" if fc == 15 else None))
            mm_done = p.val("c_t")
            p.wait("s", "c_t", mm_done)
            p.do("s", lambda eng, sems, pq=pq:
                 eng.activation(out=sb["qpre"][:, :], in_=pq[:, :], func=AT.Copy),
                 inc="c_s")
            qpre_done = p.val("c_s")
            if kind in ("q", "k"):
                p.wait("t", "c_s", qpre_done)
                p.do("t", lambda eng, sems:
                     eng.matmul(prot[:, :], sb["rotperm_b"][:, :], sb["qpre"][:, :],
                                start=True, stop=True, skip_group_check=True), inc="c_t")
                p.wait("v", "c_t")
                p.wait("v", "dtab")
                cos_off = 0 if kind == "q" else 1024
                sin_off = 512 if kind == "q" else 1536
                dst = sb["q_all"] if kind == "q" else sb["k_all"]
                p.do("v", lambda eng, sems, pq=pq, cos_off=cos_off:
                     eng.tensor_tensor(out=sb["rsc1"][:, :], in0=pq[:, :],
                                       in1=sb["tabs"][:, cos_off:cos_off + 512], op=OP.mult))
                p.do("v", lambda eng, sems, sin_off=sin_off:
                     eng.tensor_tensor(out=sb["rsc2"][:, :], in0=prot[:, :],
                                       in1=sb["tabs"][:, sin_off:sin_off + 512], op=OP.mult))
                vdrain()
                p.do("v", lambda eng, sems, dst=dst, h=h, tok0=tok0:
                     eng.tensor_tensor(out=dst[:, S * h + tok0:S * h + tok0 + 512],
                                       in0=sb["rsc1"][:, :], in1=sb["rsc2"][:, :], op=OP.add),
                     inc="c_v")
            else:
                for tt in range(4):
                    p.wait("t", "c_s", qpre_done)
                    p.wait("t", "c_v", ptv_free[0])
                    p.do("t", lambda eng, sems, tt=tt:
                         eng.transpose(ptv[:, 0:128], sb["qpre"][:, 128 * tt:128 * (tt + 1)],
                                       sb["ident_b"][:, :]), inc="c_t")
                    p.wait("v", "c_t")
                    p.do("v", lambda eng, sems, h=h, tok0=tok0, tt=tt:
                         eng.tensor_copy(out=sb["vt_all"][:, S * h + tok0 + 128 * tt:
                                                          S * h + tok0 + 128 * (tt + 1)],
                                         in_=ptv[:, 0:128]), inc="c_v")
                    ptv_free[0] = p.val("c_v")
            bank_free[of % 2] = p.val("c_v")
        panel_done_cv[pan] = p.val("c_v")
        tab_done_cv[pan] = p.val("c_v")
    p.barrier()
    st_psum2.close()
    if "q" in dmp:
        dma("y", "dout", dmp["q"], sb["q_all"][:, :])
        dma("y", "dout", dmp["k"], sb["k_all"][:, :])
        dma("y", "dout", dmp["vt"], sb["vt_all"][:, :])
        p.barrier()
    st_p2.close()

    alloc(st_p3, "e_pan", [128, 2 * S], BF16)
    alloc(st_p3, "zero_b", [128, 512], BF16)
    alloc(st_p3, "dslots", [128, 16], F32)
    alloc(st_p3, "dval", [128, 4], F32)
    alloc(st_p3, "invd_b", [128, 2], BF16)
    alloc(st_p3, "csrow", [128, 1024], F32)
    memset("g", sb["zero_b"][:, :], 0.0, inc="c_g")
    memset("v", sb["csrow"][:, :], 0.0)
    p.wait("t", "c_g")
    st_psum3 = ExitStack()
    ps0 = st_psum3.enter_context(nc.psum_tensor("ps0", [128, 512], F32))
    ps1 = st_psum3.enter_context(nc.psum_tensor("ps1", [128, 512], F32))
    ps2 = st_psum3.enter_context(nc.psum_tensor("ps2", [128, 512], F32))
    pcs0 = st_psum3.enter_context(nc.psum_tensor("pcs0", [128, 512], F32))
    pcs1 = st_psum3.enter_context(nc.psum_tensor("pcs1", [128, 512], F32))
    ptr = st_psum3.enter_context(nc.psum_tensor("ptr", [128, 512], F32))
    pbc = st_psum3.enter_context(nc.psum_tensor("pbc", [128, 512], F32))

    sq_pan = sb["e_pan"].ap()[:, 0:S]
    for h in range(NHC):
        for which in ("q", "k"):
            src = "q_all" if which == "q" else "k_all"
            p.wait("s", "c_t")
            p.do("s", lambda eng, sems, src=src, h=h:
                 eng.activation(out=sq_pan, in_=sb[src][:, S * h:S * (h + 1)],
                                func=AT.Square), inc="c_s")
            p.wait("t", "c_s")
            p.wait("t", "c_v")
            p.wait("t", "c_g")
            p.do("t", lambda eng, sems:
                 eng.matmul(pcs0[:, :], sb["ones_b"][:, 0:128], sb["zero_b"][:, :],
                            start=True, stop=True, skip_group_check=True))
            p.do("t", lambda eng, sems:
                 eng.matmul(pcs1[:, :], sb["ones_b"][:, 0:128], sb["zero_b"][:, :],
                            start=True, stop=True, skip_group_check=True))
            for kc in range(8):
                bank = (pcs0, pcs1)[kc // 4]
                j = kc % 4
                def mmn(eng, sems, bank=bank, j=j, kc=kc):
                    return eng.matmul(bank[32 * j:32 * j + 1, :],
                                      sb["ones_b"][:, 0:1],
                                      sq_pan[:, 512 * kc:512 * (kc + 1)],
                                      start=True, stop=True, skip_group_check=True,
                                      tile_position=(0, 32 * j))
                p.do("t", mmn, inc=("c_t" if kc == 7 else None))
            p.wait("v", "c_t")
            p.do("v", lambda eng, sems:
                 eng.reduce_max(sb["smax"][:, 0:1], pcs0[:, :], axis=AX))
            p.do("v", lambda eng, sems:
                 eng.reduce_max(sb["smax"][:, 1:2], pcs1[:, :], axis=AX))
            vdrain()
            p.do("v", lambda eng, sems:
                 eng.tensor_tensor(out=sb["smax"][:, 2:3], in0=sb["smax"][:, 0:1],
                                   in1=sb["smax"][:, 1:2], op=OP.max))
            vdrain()
            col = 32 * h + (0 if which == "q" else 1)
            p.do("v", lambda eng, sems, col=col:
                 eng.tensor_copy(out=sb["scr2"][:, col:col + 1], in_=sb["smax"][:, 2:3]),
                 inc="c_v")
    p.wait("t", "c_v")
    p.do("t", lambda eng, sems:
         eng.transpose(ptr[:, 0:128], sb["scr2"][:, :], sb["ident_f"][:, :]), inc="c_t")
    p.wait("v", "c_t")
    p.do("v", lambda eng, sems:
         eng.reduce_max(sb["smax"][:, 4:5], ptr[:, 0:128], axis=AX))
    vdrain()
    p.do("v", lambda eng, sems:
         eng.stream_shuffle(out=sb["smax"][:, 5:6], in_=sb["smax"][:, 4:5],
                            mask=[(i + 1) % 32 for i in range(32)]))
    vdrain()
    p.do("v", lambda eng, sems:
         eng.tensor_tensor(out=sb["smax"][:, 6:7], in0=sb["smax"][:, 4:5],
                           in1=sb["smax"][:, 5:6], op=OP.mult), inc="c_v")
    p.wait("s", "c_v")
    p.do("s", lambda eng, sems:
         eng.activation(out=sb["smax"][:, 7:8], in_=sb["smax"][:, 6:7], func=AT.Sqrt),
         inc="c_s")
    for h in range(NHC):
        p.wait("t", "c_s")
        def bcast(eng, sems, h=h):
            return eng.matmul(pbc[:, h:h + 1], sb["ones_f"][32 * h:32 * h + 1, :],
                              sb["smax"][32 * h:32 * h + 1, 7:8],
                              start=True, stop=True, skip_group_check=True,
                              tile_position=(32 * h, 0))
        p.do("t", bcast, inc=("c_t" if h == NHC - 1 else None))
    p.wait("v", "c_t")
    p.do("v", lambda eng, sems:
         eng.tensor_scalar(out=sb["negB4"][:, :], in0=pbc[:, 0:4],
                           scalar1=-1.0, scalar2=None, op0=OP.mult), inc="c_v")
    if "B4" in dmp:
        p.barrier()
        dma("y", "dout", dmp["B4"][:, 0:4], sb["negB4"][:, :])
        dma("y", "dout", dmp["B4"][:, 4:8], sb["smax"][:, 4:8])
    p.barrier()

    for h in range(NHC):
        s_bank_free = {0: 0, 1: 0, 2: 0}
        epan_free_ct = [0, 0]
        invd_free_ct = [0, 0]
        dslots_free_cv = [0, 0]
        pend = None
        p.wait("t", "c_v")
        for kc in range(8):
            bank = (pcs0, pcs1)[kc // 4]
            j = kc % 4
            def zmm(eng, sems, bank=bank, j=j):
                return eng.matmul(bank[32 * j:32 * j + 1, :], sb["ones_b"][:, 0:1],
                                  sb["zero_b"][:, :], start=True, stop=False,
                                  skip_group_check=True, tile_position=(0, 32 * j))
            p.do("t", zmm)
        def emit_colsum(qt, nkc, par):
            for kc in range(nkc):
                width = min(512, 128 * (qt + 1) - 512 * kc)
                bank = (pcs0, pcs1)[kc // 4]
                j = kc % 4
                stop = (qt == NQT - 1)
                def cmm(eng, sems, bank=bank, j=j, kc=kc, width=width, par=par, stop=stop, qt=qt):
                    epan = sb["e_pan"].ap()[:, par * S:par * S + S]
                    return eng.matmul(bank[32 * j:32 * j + 1, 0:width],
                                      sb["invd_b"][:, (qt % 2):(qt % 2) + 1],
                                      epan[:, 512 * kc:512 * kc + width],
                                      start=False, stop=stop, skip_group_check=True,
                                      tile_position=(0, 32 * j))
                p.do("t", cmm, inc=("c_t" if kc == nkc - 1 else None))
            epan_free_ct[par] = p.val("c_t")
            invd_free_ct[qt % 2] = p.val("c_t")
        for qt in range(NQT):
            nkc = (qt + 4) // 4
            par = qt % 2
            epan = sb["e_pan"].ap()[:, par * S:par * S + S]
            for kc in range(nkc):
                width = min(512, 128 * (qt + 1) - 512 * kc)
                bank = (ps0, ps1, ps2)[kc % 3]
                p.wait("t", "c_s", s_bank_free[kc % 3])
                def smm(eng, sems, bank=bank, h=h, qt=qt, kc=kc, width=width):
                    return eng.matmul(bank[:, 0:width],
                                      sb["q_all"][:, S * h + 128 * qt:S * h + 128 * (qt + 1)],
                                      sb["k_all"][:, S * h + 512 * kc:S * h + 512 * kc + width],
                                      start=True, stop=True, skip_group_check=True)
                p.do("t", smm, inc="c_t")
                mmv = p.val("c_t")
                is_diag = (kc == nkc - 1)
                if kc == 0:
                    p.wait("s", "c_t", epan_free_ct[par])
                    p.wait("s", "c_v", dslots_free_cv[par])
                if is_diag:
                    p.wait("v", "c_t", mmv)
                    p.wait("v", "dw")
                    p.do("v", lambda eng, sems, bank=bank, width=width:
                         eng.tensor_tensor(out=bank[:, width - 128:width],
                                           in0=bank[:, width - 128:width],
                                           in1=sb["tri_add"][:, :], op=OP.add), inc="c_v")
                    p.wait("s", "c_v")
                else:
                    p.wait("s", "c_t", mmv)
                def expf(eng, sems, bank=bank, epan=epan, kc=kc, width=width, h=h, par=par):
                    return eng.activation(out=epan[:, 512 * kc:512 * kc + width],
                                          in_=bank[:, 0:width], func=AT.Exp,
                                          bias=sb["negB4"][:, h:h + 1],
                                          accum_out=sb["dslots"][:, 8 * par + kc:8 * par + kc + 1])
                p.do("s", expf, inc="c_s")
                s_bank_free[kc % 3] = p.val("c_s")
            if pend is not None:
                emit_colsum(*pend)
            p.wait("v", "c_s")
            p.wait("v", "c_t", invd_free_ct[par])
            p.do("v", lambda eng, sems, nkc=nkc, par=par:
                 eng.reduce_sum(sb["dval"][:, par:par + 1],
                                sb["dslots"][:, 8 * par:8 * par + nkc], axis=AX))
            vdrain()
            p.do("v", lambda eng, sems, par=par:
                 eng.reciprocal(out=sb["dval"][:, 2 + par:3 + par],
                                in_=sb["dval"][:, par:par + 1]))
            vdrain()
            p.do("v", lambda eng, sems, par=par:
                 eng.tensor_copy(out=sb["invd_b"][:, par:par + 1],
                                 in_=sb["dval"][:, 2 + par:3 + par]), inc="c_v")
            dslots_free_cv[par] = p.val("c_v")
            p.wait("t", "c_v")
            pend = (qt, nkc, par)
        emit_colsum(*pend)
        p.wait("v", "c_t")
        for kc in range(8):
            bank = (pcs0, pcs1)[kc // 4]
            j = kc % 4
            p.do("v", lambda eng, sems, bank=bank, j=j, kc=kc:
                 eng.tensor_copy(out=sb["csrow"][32 * j:32 * j + 1,
                                                 512 * (kc // 4):512 * (kc // 4) + 512],
                                 in_=bank[32 * j:32 * j + 1, :]),
                 inc=("c_v" if kc == 7 else None))
        for kt in range(NKT):
            kc = kt // 4
            j = kc % 4
            colband = 512 * (kc // 4) + 128 * (kt % 4)
            p.wait("t", "c_v")
            p.do("t", lambda eng, sems, colband=colband:
                 eng.transpose(ptr[:, 0:128], sb["csrow"][:, colband:colband + 128],
                               sb["ident_f"][:, :]), inc="c_t")
            p.wait("v", "c_t")
            p.do("v", lambda eng, sems, h=h, kt=kt, j=j:
                 eng.tensor_copy(out=sb["cs_bias"][:, 32 * h + kt:32 * h + kt + 1],
                                 in_=ptr[:, 32 * j:32 * j + 1]), inc="c_v")
        p.wait("t", "c_v")
    if "cs" in dmp:
        p.barrier()
        dma("y", "dout", dmp["cs"], sb["cs_bias"][:, :])
    p.barrier()

    BV = sb["binv"]
    p.do("v", lambda eng, sems: eng.memset(BV[:, 0:1], 0.0))
    for h in range(NHC):
        p.do("v", lambda eng, sems, h=h:
             eng.reduce_max(sb["smax"][:, 8:9], sb["cs_bias"][:, 32 * h:32 * h + 32], axis=AX))
        vdrain()
        p.do("v", lambda eng, sems, h=h:
             eng.tensor_copy(out=sb["scr2"][:, 32 * h + 2:32 * h + 3], in_=sb["smax"][:, 8:9]),
             inc="c_v")
    p.wait("t", "c_v")
    p.do("t", lambda eng, sems:
         eng.transpose(ptr[:, 0:128], sb["scr2"][:, :], sb["ident_f"][:, :]), inc="c_t")
    p.wait("v", "c_t")
    p.do("v", lambda eng, sems:
         eng.reduce_max(sb["smax"][:, 9:10], ptr[:, 0:128], axis=AX))
    vdrain()
    p.do("v", lambda eng, sems:
         eng.stream_shuffle(out=sb["smax"][:, 10:11], in_=sb["smax"][:, 9:10],
                            mask=[(i + 2) % 32 for i in range(32)]))
    vdrain()
    p.do("v", lambda eng, sems:
         eng.tensor_scalar(out=BV[:, 1:2], in0=sb["smax"][:, 10:11], scalar1=1.0001,
                           scalar2=1e-12, op0=OP.mult, op1=OP.add))
    vdrain()
    p.do("v", lambda eng, sems:
         eng.tensor_tensor(out=BV[:, 2:3], in0=BV[:, 0:1], in1=BV[:, 1:2], op=OP.add))
    vdrain()
    p.do("v", lambda eng, sems:
         eng.tensor_scalar(out=BV[:, 2:3], in0=BV[:, 2:3], scalar1=0.5,
                           scalar2=None, op0=OP.mult), inc="c_v")
    for it in range(BIS_ITERS):
        p.wait("t", "c_v")
        for h in range(NHC):
            def bc(eng, sems, h=h):
                return eng.matmul(pbc[:, h:h + 1], sb["ones_f"][32 * h:32 * h + 1, :],
                                  BV[32 * h:32 * h + 1, 2:3],
                                  start=True, stop=True, skip_group_check=True,
                                  tile_position=(32 * h, 0))
            p.do("t", bc, inc=("c_t" if h == NHC - 1 else None))
        p.wait("v", "c_t")
        p.do("v", lambda eng, sems:
             eng.tensor_copy(out=sb["thrb"][:, :], in_=pbc[:, 0:4]))
        vdrain()
        for h in range(NHC):
            p.do("v", lambda eng, sems, h=h:
                 eng.tensor_scalar(out=sb["heavy01"][:, 32 * h:32 * h + 32],
                                   in0=sb["cs_bias"][:, 32 * h:32 * h + 32],
                                   scalar1=sb["thrb"][:, h:h + 1], scalar2=None,
                                   op0=OP.is_ge))
        for h in range(NHC):
            p.do("v", lambda eng, sems, h=h:
                 eng.reduce_sum(sb["scr2"][:, 32 * h:32 * h + 1],
                                sb["heavy01"][:, 32 * h:32 * h + 32], axis=AX),
                 inc=("c_v" if h == NHC - 1 else None))
        p.wait("t", "c_v")
        p.do("t", lambda eng, sems:
             eng.transpose(ptr[:, 0:128], sb["scr2"][:, :], sb["ident_f"][:, :]), inc="c_t")
        p.wait("v", "c_t")
        p.do("v", lambda eng, sems:
             eng.reduce_sum(BV[:, 4:5], ptr[:, 0:128], axis=AX))
        vdrain()
        p.do("v", lambda eng, sems:
             eng.tensor_scalar(out=BV[:, 3:4], in0=BV[:, 4:5], scalar1=408.5,
                               scalar2=None, op0=OP.is_ge))
        p.do("v", lambda eng, sems:
             eng.tensor_tensor(out=BV[:, 5:6], in0=BV[:, 2:3], in1=BV[:, 0:1], op=OP.subtract))
        p.do("v", lambda eng, sems:
             eng.tensor_tensor(out=BV[:, 6:7], in0=BV[:, 2:3], in1=BV[:, 1:2], op=OP.subtract))
        p.do("v", lambda eng, sems:
             eng.tensor_scalar(out=BV[:, 7:8], in0=BV[:, 3:4], scalar1=-1.0,
                               scalar2=1.0, op0=OP.mult, op1=OP.add))
        p.do("v", lambda eng, sems:
             eng.tensor_tensor(out=BV[:, 8:9], in0=BV[:, 5:6], in1=BV[:, 3:4], op=OP.mult))
        p.do("v", lambda eng, sems:
             eng.tensor_tensor(out=BV[:, 9:10], in0=BV[:, 6:7], in1=BV[:, 7:8], op=OP.mult))
        vdrain()
        p.do("v", lambda eng, sems:
             eng.tensor_tensor(out=BV[:, 0:1], in0=BV[:, 0:1], in1=BV[:, 8:9], op=OP.add))
        p.do("v", lambda eng, sems:
             eng.tensor_tensor(out=BV[:, 1:2], in0=BV[:, 1:2], in1=BV[:, 9:10], op=OP.add))
        vdrain()
        p.do("v", lambda eng, sems:
             eng.tensor_tensor(out=BV[:, 2:3], in0=BV[:, 0:1], in1=BV[:, 1:2], op=OP.add))
        vdrain()
        p.do("v", lambda eng, sems:
             eng.tensor_scalar(out=BV[:, 2:3], in0=BV[:, 2:3], scalar1=0.5,
                               scalar2=None, op0=OP.mult), inc="c_v")
    p.wait("t", "c_v")
    for h in range(NHC):
        def bc2(eng, sems, h=h):
            return eng.matmul(pbc[:, h:h + 1], sb["ones_f"][32 * h:32 * h + 1, :],
                              BV[32 * h:32 * h + 1, 0:1],
                              start=True, stop=True, skip_group_check=True,
                              tile_position=(32 * h, 0))
        p.do("t", bc2, inc=("c_t" if h == NHC - 1 else None))
    p.wait("v", "c_t")
    p.do("v", lambda eng, sems:
         eng.tensor_copy(out=sb["thrb"][:, :], in_=pbc[:, 0:4]))
    for h in range(NHC):
        p.do("v", lambda eng, sems, h=h:
             eng.tensor_scalar(out=sb["smax"][:, 12 + h:13 + h], in0=sb["negB4"][:, h:h + 1],
                               scalar1=-1e9, scalar2=None, op0=OP.add))
    for h in range(NHC):
        p.do("v", lambda eng, sems, h=h:
             eng.tensor_scalar(out=sb["heavy01"][:, 32 * h:32 * h + 32],
                               in0=sb["cs_bias"][:, 32 * h:32 * h + 32],
                               scalar1=sb["thrb"][:, h:h + 1], scalar2=None,
                               op0=OP.is_ge))
    for h in range(NHC):
        p.do("v", lambda eng, sems, h=h:
             eng.tensor_scalar(out=sb["hbias"][:, 32 * h:32 * h + 32],
                               in0=sb["heavy01"][:, 32 * h:32 * h + 32],
                               scalar1=1.0, scalar2=1e9,
                               op0=OP.subtract, op1=OP.mult))
    vdrain()
    for h in range(NHC):
        p.do("v", lambda eng, sems, h=h:
             eng.tensor_scalar(out=sb["hbias"][:, 32 * h:32 * h + 32],
                               in0=sb["hbias"][:, 32 * h:32 * h + 32],
                               scalar1=sb["negB4"][:, h:h + 1], scalar2=None,
                               op0=OP.add),
             inc=("c_v" if h == NHC - 1 else None))
    if "thr" in dmp:
        p.barrier()
        dma("y", "dout", dmp["thr"][:, 0:4], sb["thrb"][:, :])
        dma("y", "dout", dmp["thr"][:, 4:8], BV[:, 0:4])
    if "heavy" in dmp:
        p.barrier()
        dma("y", "dout", dmp["heavy"], sb["heavy01"][:, :])
    if "hb" in dmp:
        p.barrier()
        dma("y", "dout", dmp["hb"], sb["hbias"][:, :])
    p.barrier()
    st_psum3.close()

    alloc(st_p5, "attn_all", [128, NHC * S], BF16)
    alloc(st_p5, "etile", [128, 2 * 512], BF16)
    alloc(st_p5, "bcsb", [128, 512], F32)
    alloc(st_p5, "rdp", [1, 512], F32)
    st_psum5 = ExitStack()
    pst0 = st_psum5.enter_context(nc.psum_tensor("pst0", [128, 512], F32))
    pst1 = st_psum5.enter_context(nc.psum_tensor("pst1", [128, 512], F32))
    pout = st_psum5.enter_context(nc.psum_tensor("pout", [128, 512], F32))
    pd = st_psum5.enter_context(nc.psum_tensor("pd", [128, 512], F32))
    pbc5 = st_psum5.enter_context(nc.psum_tensor("pbc5", [128, 512], F32))

    st_free_s = {0: 0, 1: 0}
    et_free_t = {0: 0, 1: 0}
    pv_guard_cv = [0]
    p.wait("v", "dw")
    p.wait("s", "dw")
    for h in range(NHC):
        for qp in range(8):
            q0 = 512 * qp
            nkt = 4 * qp + 4
            mmv = {}

            def emit_S(kt, h=h, qp=qp, q0=q0, nkt=nkt):
                j0 = max(0, kt - 4 * qp)
                live0 = 128 * j0
                width = 512 - live0
                bank = (pst0, pst1)[kt % 2]
                p.wait("t", "c_s", st_free_s[kt % 2])
                if kt == 0:
                    p.wait("t", "c_v", pv_guard_cv[0])
                def stmm(eng, sems, bank=bank, h=h, kt=kt, q0=q0, live0=live0, width=width):
                    return eng.matmul(bank[:, live0:live0 + width],
                                      sb["k_all"][:, S * h + 128 * kt:S * h + 128 * (kt + 1)],
                                      sb["q_all"][:, S * h + q0 + live0:S * h + q0 + 512],
                                      start=True, stop=True, skip_group_check=True)
                p.do("t", stmm, inc="c_t")
                mmv[kt] = p.val("c_t")

            def emit_exp_masks(kt, h=h, qp=qp, nkt=nkt):
                j0 = max(0, kt - 4 * qp)
                live0 = 128 * j0
                width = 512 - live0
                bank = (pst0, pst1)[kt % 2]
                bsel = kt % 2
                subs = []
                for j in range(4):
                    if j < j0:
                        continue
                    D = 128 * (4 * qp + j - kt)
                    subs.append((j, D))
                all_far = all(D >= 640 for _, D in subs)
                esl = sb["etile"].ap()[:, 512 * (kt % 2):512 * (kt % 2) + 512]
                p.wait("s", "c_t", max(mmv[kt], et_free_t[kt % 2]))
                need_v = False
                if all_far:
                    def expb(eng, sems, bank=bank, esl=esl, h=h, kt=kt, live0=live0, width=width):
                        return eng.activation(out=esl[:, live0:live0 + width],
                                              in_=bank[:, live0:live0 + width], func=AT.Exp,
                                              bias=sb["hbias"][:, 32 * h + kt:32 * h + kt + 1])
                    p.do("s", expb, inc="c_s")
                    st_free_s[bsel] = p.val("c_s")
                else:
                    far_j = [j for j, D in subs if D >= 640]
                    near_j = [j for j, D in subs if D < 640]
                    nj0, nj1 = near_j[0], near_j[-1]
                    def expn(eng, sems, bank=bank, esl=esl, h=h, a=128 * nj0, b=128 * (nj1 + 1)):
                        return eng.activation(out=esl[:, a:b], in_=bank[:, a:b], func=AT.Exp,
                                              bias=sb["negB4"][:, h:h + 1])
                    p.do("s", expn, inc=("c_s" if not far_j else None))
                    if far_j:
                        fa, fb = 128 * far_j[0], 128 * (far_j[-1] + 1)
                        def expf2(eng, sems, bank=bank, esl=esl, h=h, kt=kt, a=fa, b=fb):
                            return eng.activation(out=esl[:, a:b], in_=bank[:, a:b], func=AT.Exp,
                                                  bias=sb["hbias"][:, 32 * h + kt:32 * h + kt + 1])
                        p.do("s", expf2, inc="c_s")
                    st_free_s[bsel] = p.val("c_s")
                    mops = []
                    for j, D in subs:
                        aa = 128 * j
                        if D == 0:
                            mops.append(lambda eng, sems, esl=esl, aa=aa:
                                        eng.tensor_tensor(out=esl[:, aa:aa + 128],
                                                          in0=esl[:, aa:aa + 128],
                                                          in1=sb["tri01_b"][:, :], op=OP.mult))
                        elif D in (384, 512):
                            mask = sb["band384_b"] if D == 384 else sb["band512_b"]
                            mops.append(lambda eng, sems, esl=esl, aa=aa, mask=mask, h=h, kt=kt:
                                        eng.scalar_tensor_tensor(
                                            out=esl[:, aa:aa + 128], in0=mask[:, :],
                                            scalar=sb["heavy01"][:, 32 * h + kt:32 * h + kt + 1],
                                            in1=esl[:, aa:aa + 128], op0=OP.max, op1=OP.mult))
                    need_v = bool(mops)
                    if need_v:
                        p.wait("v", "c_s")
                        for mi, mop in enumerate(mops):
                            p.do("v", mop, inc=("c_v" if mi == len(mops) - 1 else None))
                return need_v

            def emit_dpv(kt, need_v, h=h, qp=qp, nkt=nkt):
                j0 = max(0, kt - 4 * qp)
                live0 = 128 * j0
                width = 512 - live0
                esl = sb["etile"].ap()[:, 512 * (kt % 2):512 * (kt % 2) + 512]
                if need_v:
                    p.wait("t", "c_v")
                else:
                    p.wait("t", "c_s")
                def dmm(eng, sems, esl=esl, kt=kt, nkt=nkt, live0=live0, width=width):
                    return eng.matmul(pd[0:1, live0:live0 + width], sb["ones_b"][:, 0:1],
                                      esl[:, live0:live0 + width],
                                      start=(kt == 0), stop=(kt == nkt - 1),
                                      skip_group_check=True)
                p.do("t", dmm)
                def pvmm(eng, sems, esl=esl, h=h, kt=kt, nkt=nkt, live0=live0, width=width):
                    return eng.matmul(pout[:, live0:live0 + width],
                                      sb["vt_all"][:, S * h + 128 * kt:S * h + 128 * (kt + 1)],
                                      esl[:, live0:live0 + width],
                                      start=(kt == 0), stop=(kt == nkt - 1),
                                      skip_group_check=True)
                p.do("t", pvmm, inc="c_t")
                et_free_t[kt % 2] = p.val("c_t")

            emit_S(0)
            for kt in range(nkt):
                if kt + 1 < nkt:
                    emit_S(kt + 1)
                nv = emit_exp_masks(kt)
                emit_dpv(kt, nv)
            p.wait("v", "c_t")
            p.do("v", lambda eng, sems:
                 eng.reciprocal(out=sb["rdp"][0:1, :], in_=pd[0:1, :]), inc="c_v")
            p.wait("t", "c_v")
            p.do("t", lambda eng, sems:
                 eng.matmul(pbc5[:, :], sb["ones_f"][0:1, :], sb["rdp"][0:1, :],
                            start=True, stop=True, skip_group_check=True), inc="c_t")
            p.wait("s", "c_t")
            p.do("s", lambda eng, sems:
                 eng.activation(out=sb["bcsb"][:, :], in_=pbc5[:, :], func=AT.Copy),
                 inc="c_s")
            p.wait("v", "c_s")
            p.do("v", lambda eng, sems, h=h, q0=q0:
                 eng.tensor_tensor(out=sb["attn_all"][:, S * h + q0:S * h + q0 + 512],
                                   in0=pout[:, :], in1=sb["bcsb"][:, :], op=OP.mult),
                 inc="c_v")
            pv_guard_cv[0] = p.val("c_v")
    if "attn" in dmp:
        p.barrier()
        dma("y", "dout", dmp["attn"], sb["attn_all"][:, :])
    p.barrier()
    st_psum5.close()

    for h in range(NHC):
        dma("y", "dcc", cc_in.ap()[128 * h:128 * (h + 1), :],
            sb["attn_all"][:, S * h:S * (h + 1)])
    p.wait("g", "dcc")
    def ccall(eng, sems):
        return eng.collective_compute(
            "AllGather", OP.bypass,
            replica_groups=[[0, 1, 2, 3], [4, 5, 6, 7]],
            ins=[cc_in.ap().opt()],
            outs=[cc_out.ap().opt()],
        )
    p.do("g", ccall, inc="cc_done")
    p.barrier()
    st_p5.close()
    st_p3.close()
    st_qkv.close()

    alloc(st_p6a, "h2full", [128, 2 * 32 * 512], BF16)
    st_p6r = ExitStack()
    alloc(st_p6r, "xbf0", [128, DM], F32)
    alloc(st_p6r, "xbf1", [128, DM], F32)
    alloc(st_p6r, "sqd6", [128, DM], BF16)
    alloc(st_p6r, "xno0", [128, DM], BF16)
    alloc(st_p6r, "xno1", [128, DM], BF16)
    st_psum6 = ExitStack()
    ph0 = st_psum6.enter_context(nc.psum_tensor("ph0", [128, 512], F32))
    ph1 = st_psum6.enter_context(nc.psum_tensor("ph1", [128, 512], F32))

    for e_ in ("v", "s", "g", "t", "y"):
        p.wait(e_, "dcc")

    ln_tiles(xch_in, TCH // 128, xnch_scr, "dxf", "sqd6", (sb["xno0"], sb["xno1"]),
             (sb["xbf0"], sb["xbf1"]))
    p.barrier()

    st_p6r.close()
    alloc(st_p6b, "xn_ffn", [128, 32 * 512], BF16)
    alloc(st_p6b, "h1full", [128, 2 * 32 * 512], BF16)
    alloc(st_p6b, "w1c0", [128, 16 * 128], BF16)
    alloc(st_p6b, "w1c1", [128, 16 * 128], BF16)
    alloc(st_p6b, "w2c0", [128, 32 * 128], BF16)
    alloc(st_p6b, "w2c1", [128, 32 * 128], BF16)
    ffn_bank_free = {0: 0, 1: 0}
    w1_used_ct = [0, 0]
    w2_used_ct = [0, 0]
    p.wait("y", "dxf")
    for th in range(2):
        for fc in range(16):
            dma("y", "dxr", sb["xn_ffn"][:, 1024 * fc + 512 * th:1024 * fc + 512 * (th + 1)],
                xnch_scr.ap()[512 * th:512 * (th + 1), 128 * fc:128 * (fc + 1)], transpose=True)
    if "xnf" in dmp:
        p.wait("y", "dxr")
        dma("y", "dout", dmp["xnf"], sb["xn_ffn"][:, 0:16 * 512])
    for hf in range(32):
        wbuf = (sb["w1c0"], sb["w1c1"])[hf % 2]
        p.wait("g", "c_t", w1_used_ct[hf % 2])
        dma("g", "dwf", wbuf[:, :], w1_in[128 * hf:128 * (hf + 1), :])
        p.wait("t", "dwf")
        p.wait("t", "dxr")
        for tp in range(2):
            bank = (ph0, ph1)[tp]
            p.wait("t", "c_s", ffn_bank_free[tp])
            for fc in range(16):
                def hmm(eng, sems, bank=bank, wbuf=wbuf, fc=fc, tp=tp,
                        first=(fc == 0), last=(fc == 15)):
                    return eng.matmul(bank[:, :], wbuf[:, 128 * fc:128 * (fc + 1)],
                                      sb["xn_ffn"][:, 1024 * fc + 512 * tp:1024 * fc + 512 * (tp + 1)],
                                      start=first, stop=last, skip_group_check=True)
                p.do("t", hmm, inc=("c_t" if fc == 15 else None))
            p.wait("s", "c_t")
            p.do("s", lambda eng, sems, bank=bank, hf=hf, tp=tp:
                 eng.activation(out=sb["h1full"][:, 1024 * hf + 512 * tp:1024 * hf + 512 * (tp + 1)],
                                in_=bank[:, :],
                                func=AT.Relu, bias=sb["b1_sb"][:, hf:hf + 1]), inc="c_s")
            ffn_bank_free[tp] = p.val("c_s")
        w1_used_ct[hf % 2] = p.val("c_t")
    if "h1" in dmp:
        p.barrier()
        dma("y", "dout", dmp["h1"], sb["h1full"][:, 0:16 * 1024])
    p.barrier()
    for hf2 in range(32):
        wbuf = (sb["w2c0"], sb["w2c1"])[hf2 % 2]
        p.wait("g", "c_t", w2_used_ct[hf2 % 2])
        dma("g", "dwf", wbuf[:, :], w2_in[128 * hf2:128 * (hf2 + 1), :])
        p.wait("t", "dwf")
        for tp in range(2):
            bank = (ph0, ph1)[tp]
            p.wait("t", "c_s", ffn_bank_free[tp])
            for fc in range(32):
                def h2mm(eng, sems, bank=bank, wbuf=wbuf, fc=fc, tp=tp,
                         first=(fc == 0), last=(fc == 31)):
                    return eng.matmul(bank[:, :], wbuf[:, 128 * fc:128 * (fc + 1)],
                                      sb["h1full"][:, 1024 * fc + 512 * tp:1024 * fc + 512 * (tp + 1)],
                                      start=first, stop=last, skip_group_check=True)
                p.do("t", h2mm, inc=("c_t" if fc == 31 else None))
            p.wait("s", "c_t")
            p.do("s", lambda eng, sems, bank=bank, hf2=hf2, tp=tp:
                 eng.activation(out=sb["h2full"][:, 1024 * hf2 + 512 * tp:1024 * hf2 + 512 * (tp + 1)],
                                in_=bank[:, :],
                                func=AT.Relu, bias=sb["b2_sb"][:, hf2:hf2 + 1]), inc="c_s")
            ffn_bank_free[tp] = p.val("c_s")
        w2_used_ct[hf2 % 2] = p.val("c_t")
    if "h2" in dmp:
        p.wait("y", "c_s")
        dma("y", "dout", dmp["h2"], sb["h2full"][:, :])
    p.barrier()
    st_psum6.close()
    st_p6b.close()

    alloc(st_p6b, "agat", [128, 32 * 512], BF16)
    alloc(st_p6b, "gtmp0", [128, S], BF16)
    alloc(st_p6b, "gtmp1", [128, S], BF16)
    alloc(st_p6b, "woc0", [128, 16 * 256], BF16)
    alloc(st_p6b, "woc1", [128, 16 * 256], BF16)
    alloc(st_p6b, "w3c0", [128, 32 * 256], BF16)
    alloc(st_p6b, "w3c1", [128, 32 * 256], BF16)
    alloc(st_p6b, "b3_sb", [1, DM], BF16)
    alloc(st_p6b, "xsl0", [128, 256], F32)
    alloc(st_p6b, "xsl1", [128, 256], F32)
    alloc(st_p6b, "osl0", [128, 256], F32)
    alloc(st_p6b, "osl1", [128, 256], F32)
    st_psum7 = ExitStack()
    pa0 = st_psum7.enter_context(nc.psum_tensor("pa0", [128, 512], F32))
    pa1 = st_psum7.enter_context(nc.psum_tensor("pa1", [128, 512], F32))

    dma("g", "dwf", sb["b3_sb"][:, :], b3_in)
    p.wait("y", "cc_done", 1)
    gsel_done_cv = [0, 0]
    for afp in range(8):
        af0, af1 = 2 * afp, 2 * afp + 1
        p.wait("y", "c_v", gsel_done_cv[0])
        dma("y", "dag", sb["gtmp0"][:, :], cc_out.ap()[128 * af0:128 * (af0 + 1), :])
        p.wait("y", "c_v", gsel_done_cv[1])
        dma("y", "dag", sb["gtmp1"][:, :], cc_out.ap()[128 * af1:128 * (af1 + 1), :])
        p.wait("v", "dag")
        for cch in range(4):
            if cch > 0:
                vdrain()
            for gi, (gb, af) in enumerate(((sb["gtmp0"], af0), (sb["gtmp1"], af1))):
                if cch == 0:
                    p.do("v", lambda eng, sems, gb=gb, af=af:
                         eng.tensor_scalar(out=sb["agat"][:, 1024 * af:1024 * (af + 1)],
                                           in0=gb[:, 0:TCH], scalar1=sb["selm"][:, 0:1],
                                           scalar2=None, op0=OP.mult))
                else:
                    p.do("v", lambda eng, sems, gb=gb, af=af, cch=cch:
                         eng.scalar_tensor_tensor(out=sb["agat"][:, 1024 * af:1024 * (af + 1)],
                                                  in0=gb[:, TCH * cch:TCH * (cch + 1)],
                                                  scalar=sb["selm"][:, cch:cch + 1],
                                                  in1=sb["agat"][:, 1024 * af:1024 * (af + 1)],
                                                  op0=OP.mult, op1=OP.add),
                         inc=("c_v" if (cch == 3 and gi == 1) else None))
        gsel_done_cv[0] = p.val("c_v")
        gsel_done_cv[1] = p.val("c_v")
    p.barrier()
    if "agat" in dmp:
        p.wait("y", "c_v")
        dma("y", "dout", dmp["agat"], sb["agat"][:, :])
    as_bank_free = {0: 0, 1: 0}
    wo_used_ct = [0, 0]
    x_used_cv = [0, 0]
    out_dma_done = [0, 0]
    for oc in range(8):
        wob = (sb["woc0"], sb["woc1"])[oc % 2]
        w3b = (sb["w3c0"], sb["w3c1"])[oc % 2]
        p.wait("g", "c_t", wo_used_ct[oc % 2])
        dma("g", "dwf", wob[:, :], wo_in[128 * oc:128 * (oc + 1), :])
        dma("g", "dwf", w3b[:, :], w3_in[128 * oc:128 * (oc + 1), :])
        p.wait("t", "dwf")
        p.wait("t", "dag")
        for tt in range(8):
            xb = (sb["xsl0"], sb["xsl1"])[tt % 2]
            ob = (sb["osl0"], sb["osl1"])[tt % 2]
            p.wait("y", "c_v", x_used_cv[tt % 2])
            dma("y", "dx", xb[:, :],
                xch_in[128 * tt:128 * (tt + 1), 256 * oc:256 * (oc + 1)])
            bank = (pa0, pa1)[tt % 2]
            p.wait("t", "c_v", as_bank_free[tt % 2])
            def b3mm(eng, sems, bank=bank, oc=oc):
                return eng.matmul(bank[:, 0:256], sb["ones_b"][0:1, 0:128],
                                  sb["b3_sb"][0:1, 256 * oc:256 * (oc + 1)],
                                  start=True, stop=False, skip_group_check=True)
            p.do("t", b3mm)
            for fc in range(16):
                def womm(eng, sems, bank=bank, wob=wob, fc=fc, tt=tt):
                    return eng.matmul(bank[:, 0:256],
                                      sb["agat"][:, 1024 * fc + 128 * tt:1024 * fc + 128 * (tt + 1)],
                                      wob[:, 256 * fc:256 * (fc + 1)],
                                      start=False, stop=False, skip_group_check=True)
                p.do("t", womm)
            for fc in range(32):
                def w3mm(eng, sems, bank=bank, w3b=w3b, fc=fc, tt=tt, last=(fc == 31)):
                    return eng.matmul(bank[:, 0:256],
                                      sb["h2full"][:, 1024 * fc + 512 * (tt // 4) + 128 * (tt % 4):
                                                   1024 * fc + 512 * (tt // 4) + 128 * (tt % 4 + 1)],
                                      w3b[:, 256 * fc:256 * (fc + 1)],
                                      start=False, stop=last, skip_group_check=True)
                p.do("t", w3mm, inc=("c_t" if fc == 31 else None))
            p.wait("v", "c_t")
            p.wait("v", "dx")
            p.wait("v", "dout", out_dma_done[tt % 2])
            p.do("v", lambda eng, sems, bank=bank, xb=xb, ob=ob:
                 eng.tensor_tensor(out=ob[:, :], in0=bank[:, 0:256],
                                   in1=xb[:, :], op=OP.add), inc="c_v")
            as_bank_free[tt % 2] = p.val("c_v")
            x_used_cv[tt % 2] = p.val("c_v")
            p.wait("y", "c_v")
            dma("y", "dout", out_c[128 * tt:128 * (tt + 1), 256 * oc:256 * (oc + 1)],
                ob[:, :])
            out_dma_done[tt % 2] = p.val("dout")
        wo_used_ct[oc % 2] = p.val("c_t")
    p.wait("y", "dout")
    p.barrier()
    st_psum7.close()
    st_p6b.close()
    st_p6a.close()
    st_perm.close()

    with ExitStack() as bctx:
        sems = {}
        for s_ in p.sem_names:
            sems[s_] = bctx.enter_context(nc.semaphore(s_))
        block = bctx.enter_context(nc.Block())

        @block.tensor
        def _(eng):
            p.replay("t", eng, sems)

        @block.vector
        def _(eng):
            p.replay("v", eng, sems)

        @block.scalar
        def _(eng):
            p.replay("s", eng, sems)

        @block.gpsimd
        def _(eng):
            p.replay("g", eng, sems)

        @block.sync
        def _(eng):
            p.replay("y", eng, sems)

    return nc, dmp


def _make_inputs(full):
    x = np.ascontiguousarray(full["L0_attn_hidden_states"])
    cst = _host_constants()
    maps = []
    for core in range(8):
        b, hg = core // 4, core % 4
        m = {
            "x_b": np.ascontiguousarray(x[b]),
            "x_chunk": np.ascontiguousarray(x[b, TCH * hg:TCH * (hg + 1), :]),
            "wq_t": np.ascontiguousarray(full["L0_attn_Wq"][512 * hg:512 * (hg + 1), :].T),
            "wk_t": np.ascontiguousarray(full["L0_attn_Wk"][512 * hg:512 * (hg + 1), :].T),
            "wv_t": np.ascontiguousarray(full["L0_attn_Wv"][512 * hg:512 * (hg + 1), :].T),
            "wo_tl": np.ascontiguousarray(
                full["L0_attn_Wo"].reshape(8, 256, 16, 128).transpose(0, 3, 2, 1)
                .reshape(1024, 4096)),
            "w1_tl": np.ascontiguousarray(
                full["L0_ffn_W1"].reshape(32, 128, 16, 128).transpose(0, 3, 2, 1)
                .reshape(FF, DM)),
            "w2_tl": np.ascontiguousarray(
                full["L0_ffn_W2"].reshape(32, 128, 32, 128).transpose(0, 3, 2, 1)
                .reshape(FF, FF)),
            "w3_tl": np.ascontiguousarray(
                full["L0_ffn_W3"].reshape(8, 256, 32, 128).transpose(0, 3, 2, 1)
                .reshape(1024, 8192)),
            "b1s": np.ascontiguousarray(full["L0_ffn_b1"].reshape(32, 128).T),
            "b2s": np.ascontiguousarray(full["L0_ffn_b2"].reshape(32, 128).T),
            "b3r": np.ascontiguousarray(full["L0_ffn_b3"].reshape(1, DM)),
            "selmask": np.ascontiguousarray(
                np.repeat((np.arange(4) == hg).astype(np.float32)[None, :], 128, 0)),
        }
        for k, v in cst.items():
            m["c_" + k] = v
        maps.append(m)
    return maps


def kernel(**inputs):
    nc, _ = build_program(dump=_DUMP)
    maps = _make_inputs(inputs)
    res = run_bass_kernel_spmd(nc, maps, core_ids=list(range(8)))
    out = np.empty((2, S, DM), dtype=np.float32)
    for core in range(8):
        b, hg = core // 4, core % 4
        out[b, TCH * hg:TCH * (hg + 1), :] = res.results[core]["out_c"]
    return out


# version 20
# speedup vs baseline: 1.0028x; 1.0028x over previous
import math
from contextlib import ExitStack

import numpy as np

import concourse.bass as bass
import concourse.mybir as mybir
from concourse.bass_utils import run_bass_kernel_spmd

F32 = mybir.dt.float32
BF16 = mybir.dt.bfloat16
AT = mybir.ActivationFunctionType
OP = mybir.AluOpType
AX = mybir.AxisListType.X

S = 4096
DM = 2048
HD = 128
NHC = 4
FF = 4096
TCH = 1024
NEG = -1.0e9
NKT = S // 128
NQT = S // 128
BIS_ITERS = 40
EPS = 1e-5

_DUMP = ()


class Prog:
    ENGS = ("t", "v", "s", "g", "y")

    def __init__(self):
        self.ops = {e: [] for e in self.ENGS}
        self.sem_names = []
        self.counts = {}

    def sem(self, name):
        self.sem_names.append(name)
        self.counts[name] = 0
        return name

    def do(self, eng, fn, inc=None, inc_n=1):
        if inc is not None:
            self.counts[inc] += inc_n
        self.ops[eng].append(("do", fn, inc, inc_n))

    def wait(self, eng, sem, val=None):
        v = self.counts[sem] if val is None else val
        self.ops[eng].append(("wait", sem, v))

    def val(self, sem):
        return self.counts[sem]

    def barrier(self, engs=None):
        engs = engs or self.ENGS
        for e in engs:
            self.do(e, lambda eng, sems: eng.nop(), inc="bar")
        tot = self.counts["bar"]
        for e in engs:
            self.wait(e, "bar", tot)

    def replay(self, eng_name, eng, sems):
        for item in self.ops[eng_name]:
            if item[0] == "do":
                _, fn, inc, inc_n = item
                instr = fn(eng, sems)
                if inc is not None:
                    assert instr is not None
                    instr.then_inc(sems[inc], inc_n)
            else:
                _, sem, v = item
                if v > 0:
                    eng.wait_ge(sems[sem], v)


def _host_constants():
    c = {}
    c["ident"] = np.eye(128, dtype=np.float32)
    rp = np.zeros((128, 128), dtype=np.float32)
    for pp in range(128):
        rp[(pp + 64) % 128, pp] = 1.0
    c["rotperm"] = rp
    i = np.arange(128)
    c["tri_add_qk"] = np.where(i[None, :] <= i[:, None], 0.0, NEG).astype(np.float32)
    c["tri01_kq"] = (i[:, None] <= i[None, :]).astype(np.float32)
    for D in (384, 512):
        c["band01_%d" % D] = ((D + i[None, :] - i[:, None]) <= 409).astype(np.float32)
    inv = (1.0 / (10000.0 ** (np.arange(0, HD, 2, dtype=np.float32) / HD))).astype(np.float32)
    pos = np.arange(S, dtype=np.float32)
    fr = (pos[:, None] * inv[None, :]).astype(np.float32)
    cos = np.cos(fr).astype(np.float32)
    sin = np.sin(fr).astype(np.float32)
    cos_t = np.ascontiguousarray(np.concatenate([cos, cos], axis=1).T)
    sin_full = np.ascontiguousarray(np.concatenate([sin, sin], axis=1).T)
    sgn = np.where(np.arange(128) < 64, -1.0, 1.0).astype(np.float32)[:, None]
    sin_star = (sin_full * sgn).astype(np.float32)
    sc = np.float32(1.0 / math.sqrt(HD))
    c["cos_k"] = cos_t
    c["sin_k"] = sin_star
    c["cos_q"] = (cos_t * sc).astype(np.float32)
    c["sin_q"] = (sin_star * sc).astype(np.float32)
    return c


def build_program(dump=()):
    nc = bass.Bass("TRN2", target_bir_lowering=False, debug=False)
    p = Prog()

    x_in = nc.dram_tensor("x_b", [S, DM], F32, kind="ExternalInput").ap()
    xch_in = nc.dram_tensor("x_chunk", [TCH, DM], F32, kind="ExternalInput").ap()
    wq_in = nc.dram_tensor("wq_t", [DM, 512], F32, kind="ExternalInput").ap()
    wk_in = nc.dram_tensor("wk_t", [DM, 512], F32, kind="ExternalInput").ap()
    wv_in = nc.dram_tensor("wv_t", [DM, 512], F32, kind="ExternalInput").ap()
    wo_in = nc.dram_tensor("wo_tl", [1024, 4096], F32, kind="ExternalInput").ap()
    w1_in = nc.dram_tensor("w1_tl", [FF, DM], F32, kind="ExternalInput").ap()
    w2_in = nc.dram_tensor("w2_tl", [FF, FF], F32, kind="ExternalInput").ap()
    w3_in = nc.dram_tensor("w3_tl", [1024, 8192], F32, kind="ExternalInput").ap()
    b1_in = nc.dram_tensor("b1s", [128, 32], F32, kind="ExternalInput").ap()
    b2_in = nc.dram_tensor("b2s", [128, 32], F32, kind="ExternalInput").ap()
    b3_in = nc.dram_tensor("b3r", [1, DM], F32, kind="ExternalInput").ap()
    sel_in = nc.dram_tensor("selmask", [128, 4], F32, kind="ExternalInput").ap()
    cst = {}
    for name, shape in [("ident", [128, 128]), ("rotperm", [128, 128]),
                        ("tri_add_qk", [128, 128]), ("tri01_kq", [128, 128]),
                        ("band01_384", [128, 128]), ("band01_512", [128, 128]),
                        ("cos_q", [128, S]), ("sin_q", [128, S]),
                        ("cos_k", [128, S]), ("sin_k", [128, S])]:
        cst[name] = nc.dram_tensor("c_" + name, shape, F32, kind="ExternalInput").ap()

    out_c = nc.dram_tensor("out_c", [TCH, DM], F32, kind="ExternalOutput").ap()

    xn_scr = nc.dram_tensor("xn_scr", [S, DM], BF16, kind="Internal")
    xnch_scr = nc.dram_tensor("xnch_scr", [TCH, DM], BF16, kind="Internal")
    cc_in = nc.dram_tensor("cc_in", [512, S], BF16, kind="Internal")
    cc_out = nc.dram_tensor("cc_out", [DM, S], BF16, kind="Internal")

    dump = set(dump)
    dmp = {}
    def _dump_decl(name, shape, dtype=F32):
        if name in dump:
            dmp[name] = nc.dram_tensor("dmp_" + name, shape, dtype, kind="ExternalOutput").ap()
    _dump_decl("xn", [S, DM], BF16)
    _dump_decl("q", [128, NHC * S], BF16)
    _dump_decl("k", [128, NHC * S], BF16)
    _dump_decl("vt", [128, NHC * S], BF16)
    _dump_decl("B4", [128, 8])
    _dump_decl("cs", [128, 128])
    _dump_decl("thr", [128, 8])
    _dump_decl("heavy", [128, 128])
    _dump_decl("hb", [128, 128])
    _dump_decl("attn", [128, NHC * S], BF16)
    _dump_decl("h1", [128, 32 * 512], BF16)
    _dump_decl("h1nb", [128, 32 * 512], BF16)
    _dump_decl("h2", [128, 64 * 512], BF16)
    _dump_decl("h2end", [128, 64 * 512], BF16)
    _dump_decl("agend", [128, 32 * 512], BF16)
    _dump_decl("agat", [128, 32 * 512], BF16)
    _dump_decl("xnf", [128, 16 * 512], BF16)

    for s_ in ["bar", "dx", "dxw", "dxr", "dtab", "dw", "dwy", "dwf", "dcc", "dout", "dag", "dxf",
               "c_t", "c_v", "c_s", "c_g", "cc_done"]:
        p.sem(s_)

    st_perm = ExitStack()
    st_qkv = ExitStack()
    st_p1 = ExitStack()
    st_p2 = ExitStack()
    st_p3 = ExitStack()
    st_p5 = ExitStack()
    st_p6a = ExitStack()
    st_p6b = ExitStack()

    sb = {}
    def alloc(stack, name, shape, dtype):
        sb[name] = stack.enter_context(nc.sbuf_tensor(name, shape, dtype))
        return sb[name]

    alloc(st_perm, "ident_f", [128, 128], F32)
    alloc(st_perm, "ident_b", [128, 128], BF16)
    alloc(st_perm, "rotperm_b", [128, 128], BF16)
    alloc(st_perm, "tri_add", [128, 128], F32)
    alloc(st_perm, "tri01_b", [128, 128], BF16)
    alloc(st_perm, "band384_b", [128, 128], BF16)
    alloc(st_perm, "band512_b", [128, 128], BF16)
    alloc(st_perm, "ones_b", [128, 128], BF16)
    alloc(st_perm, "ones_f", [128, 128], F32)
    alloc(st_perm, "selm", [128, 4], F32)
    alloc(st_perm, "b1_sb", [128, 32], F32)
    alloc(st_perm, "b2_sb", [128, 32], F32)
    alloc(st_perm, "cs_bias", [128, 128], F32)
    alloc(st_perm, "heavy01", [128, 128], F32)
    alloc(st_perm, "hbias", [128, 128], F32)
    alloc(st_perm, "negB4", [128, 4], F32)
    alloc(st_perm, "scr2", [128, 128], F32)
    alloc(st_perm, "smax", [128, 16], F32)
    alloc(st_perm, "binv", [128, 16], F32)
    alloc(st_perm, "thrb", [128, 4], F32)
    alloc(st_perm, "lnst", [128, 8], F32)

    alloc(st_qkv, "q_all", [128, NHC * S], BF16)
    alloc(st_qkv, "k_all", [128, NHC * S], BF16)
    alloc(st_qkv, "vt_all", [128, NHC * S], BF16)

    alloc(st_p1, "xbuf0", [128, DM], F32)
    alloc(st_p1, "xbuf1", [128, DM], F32)
    alloc(st_p1, "sqdump", [128, DM], BF16)
    alloc(st_p1, "xnout0", [128, DM], BF16)
    alloc(st_p1, "xnout1", [128, DM], BF16)

    def dma(eng_key, sem, dst, src, **kw):
        def f(eng, sems, dst=dst, src=src, kw=kw):
            return eng.dma_start(out=dst, in_=src, **kw)
        p.do(eng_key, f, inc=sem, inc_n=16)

    def vdrain():
        p.do("v", lambda eng, sems: eng.drain())

    def memset(eng_key, dst, v, inc=None):
        def f(eng, sems, dst=dst, v=v):
            return eng.memset(dst, v)
        p.do(eng_key, f, inc=inc)

    dma("g", "dw", sb["ident_b"][:, :], cst["ident"])
    dma("g", "dw", sb["rotperm_b"][:, :], cst["rotperm"])
    dma("g", "dw", sb["tri01_b"][:, :], cst["tri01_kq"])
    dma("g", "dw", sb["band384_b"][:, :], cst["band01_384"])
    dma("g", "dw", sb["band512_b"][:, :], cst["band01_512"])
    dma("y", "dwy", sb["ident_f"][:, :], cst["ident"])
    dma("y", "dwy", sb["tri_add"][:, :], cst["tri_add_qk"])
    dma("y", "dwy", sb["selm"][:, :], sel_in)
    dma("y", "dwy", sb["b1_sb"][:, :], b1_in)
    dma("y", "dwy", sb["b2_sb"][:, :], b2_in)
    memset("g", sb["ones_b"][:, :], 1.0)
    memset("g", sb["ones_f"][:, :], 1.0)
    memset("g", sb["scr2"][:, :], 0.0)
    memset("v", sb["cs_bias"][:, :], 0.0)
    p.barrier()

    def ln_tiles(src_ap, n_tiles, dst_scr, wsem, sq_buf_name, xn_bufs, xbufs):
        last_apply_cv = [0, 0]
        for it in range(n_tiles):
            xb = xbufs[it % 2]
            xnob = xn_bufs[it % 2]
            if it >= 2:
                p.wait("y", wsem, 16 * (it - 1))
            dma("y", "dx", xb[:, :], src_ap[128 * it:128 * (it + 1), :])
            p.wait("v", "dx")
            p.wait("s", "dx")
            if it >= 1:
                p.wait("s", "c_v", last_apply_cv[0])
            ln = sb["lnst"]
            p.do("v", lambda eng, sems, xb=xb, ln=ln:
                 eng.reduce_sum(ln[:, 0:1], xb[:, :], axis=AX), inc="c_v")
            p.do("s", lambda eng, sems, xb=xb, ln=ln:
                 eng.activation(out=sb[sq_buf_name][:, 0:DM], in_=xb[:, :], func=AT.Square,
                                accum_out=ln[:, 1:2]), inc="c_s")
            p.wait("v", "c_s")
            vdrain()
            p.do("v", lambda eng, sems, ln=ln:
                 eng.tensor_scalar(out=ln[:, 2:3], in0=ln[:, 0:1], scalar1=1.0 / DM,
                                   scalar2=None, op0=OP.mult))
            p.do("v", lambda eng, sems, ln=ln:
                 eng.tensor_scalar(out=ln[:, 3:4], in0=ln[:, 1:2], scalar1=1.0 / DM,
                                   scalar2=None, op0=OP.mult))
            p.do("v", lambda eng, sems, ln=ln:
                 eng.tensor_tensor(out=ln[:, 4:5], in0=ln[:, 2:3], in1=ln[:, 2:3], op=OP.mult))
            vdrain()
            p.do("v", lambda eng, sems, ln=ln:
                 eng.tensor_tensor(out=ln[:, 5:6], in0=ln[:, 3:4], in1=ln[:, 4:5], op=OP.subtract))
            vdrain()
            p.do("v", lambda eng, sems, ln=ln:
                 eng.tensor_scalar(out=ln[:, 5:6], in0=ln[:, 5:6], scalar1=float(EPS),
                                   scalar2=None, op0=OP.add))
            vdrain()
            p.do("v", lambda eng, sems, ln=ln:
                 eng.reciprocal(out=ln[:, 6:7], in_=ln[:, 5:6]), inc="c_v")
            p.wait("s", "c_v")
            p.do("s", lambda eng, sems, ln=ln:
                 eng.activation(out=ln[:, 6:7], in_=ln[:, 6:7], func=AT.Sqrt), inc="c_s")
            p.wait("v", "c_s")
            p.do("v", lambda eng, sems, ln=ln:
                 eng.tensor_tensor(out=ln[:, 7:8], in0=ln[:, 2:3], in1=ln[:, 6:7], op=OP.mult))
            vdrain()
            p.do("v", lambda eng, sems, ln=ln:
                 eng.tensor_scalar(out=ln[:, 7:8], in0=ln[:, 7:8], scalar1=-1.0,
                                   scalar2=None, op0=OP.mult))
            if it >= 2:
                p.wait("v", wsem, 16 * (it - 1))
            vdrain()
            p.do("v", lambda eng, sems, xb=xb, ln=ln, xnob=xnob:
                 eng.tensor_scalar(out=xnob[:, :], in0=xb[:, :],
                                   scalar1=ln[:, 6:7], scalar2=ln[:, 7:8],
                                   op0=OP.mult, op1=OP.add), inc="c_v")
            last_apply_cv[0] = p.val("c_v")
            p.wait("y", "c_v")
            dma("y", wsem, dst_scr.ap()[128 * it:128 * (it + 1), :], xnob[:, :])

    ln_tiles(x_in, S // 128, xn_scr, "dxw", "sqdump", (sb["xnout0"], sb["xnout1"]),
             (sb["xbuf0"], sb["xbuf1"]))
    if "xn" in dmp:
        p.wait("y", "dxw")
        dma("y", "dout", dmp["xn"], xn_scr.ap()[:, :])
    p.barrier()

    st_p1.close()
    alloc(st_p2, "wq_sb", [128, 16 * 512], BF16)
    alloc(st_p2, "wk_sb", [128, 16 * 512], BF16)
    alloc(st_p2, "wv_sb", [128, 16 * 512], BF16)
    alloc(st_p2, "xnp0", [128, 16 * 512], BF16)
    alloc(st_p2, "xnp1", [128, 16 * 512], BF16)
    alloc(st_p2, "tabs", [128, 4 * 512], F32)
    alloc(st_p2, "qpre", [128, 512], BF16)
    alloc(st_p2, "rsc1", [128, 512], F32)
    alloc(st_p2, "rsc2", [128, 512], F32)
    st_psum2 = ExitStack()
    for nm, wsrc in (("wq_sb", wq_in), ("wk_sb", wk_in), ("wv_sb", wv_in)):
        for fc in range(16):
            dma("g", "dw", sb[nm][:, 512 * fc:512 * (fc + 1)],
                wsrc[128 * fc:128 * (fc + 1), :])
    pq0 = st_psum2.enter_context(nc.psum_tensor("pq0", [128, 512], F32))
    pq1 = st_psum2.enter_context(nc.psum_tensor("pq1", [128, 512], F32))
    prot = st_psum2.enter_context(nc.psum_tensor("prot", [128, 512], F32))
    ptv = st_psum2.enter_context(nc.psum_tensor("ptv", [128, 1024], BF16))

    bank_free = {0: 0, 1: 0}
    panel_done_cv = {}
    tab_done_cv = {}
    ptv_free = [0]
    for pan in range(8):
        xnp = (sb["xnp0"], sb["xnp1"])[pan % 2]
        tok0 = 512 * pan
        if pan >= 2:
            p.wait("y", "c_v", panel_done_cv[pan - 2])
        for fc in range(16):
            dma("y", "dxr", xnp[:, 512 * fc:512 * (fc + 1)],
                xn_scr.ap()[tok0:tok0 + 512, 128 * fc:128 * (fc + 1)], transpose=True)
        if pan >= 1:
            p.wait("y", "c_v", tab_done_cv[pan - 1])
        for ti, nm in enumerate(("cos_q", "sin_q", "cos_k", "sin_k")):
            dma("y", "dtab", sb["tabs"][:, 512 * ti:512 * (ti + 1)],
                cst[nm][:, tok0:tok0 + 512])
        p.wait("t", "dxr")
        p.wait("t", "dw")
        p.wait("t", "dwy")
        for of in range(12):
            kind = "q" if of < 4 else ("k" if of < 8 else "v")
            h = of % 4
            wname = {"q": "wq_sb", "k": "wk_sb", "v": "wv_sb"}[kind]
            pq = (pq0, pq1)[of % 2]
            p.wait("t", "c_v", bank_free[of % 2])
            for fc in range(16):
                def mm(eng, sems, pq=pq, wname=wname, fc=fc, h=h, xnp=xnp,
                       first=(fc == 0), last=(fc == 15)):
                    return eng.matmul(pq[:, :],
                                      sb[wname][:, 512 * fc + 128 * h:512 * fc + 128 * (h + 1)],
                                      xnp[:, 512 * fc:512 * (fc + 1)],
                                      start=first, stop=last, skip_group_check=True)
                p.do("t", mm, inc=("c_t" if fc == 15 else None))
            mm_done = p.val("c_t")
            p.wait("s", "c_t", mm_done)
            p.do("s", lambda eng, sems, pq=pq:
                 eng.activation(out=sb["qpre"][:, :], in_=pq[:, :], func=AT.Copy),
                 inc="c_s")
            qpre_done = p.val("c_s")
            if kind in ("q", "k"):
                p.wait("t", "c_s", qpre_done)
                p.do("t", lambda eng, sems:
                     eng.matmul(prot[:, :], sb["rotperm_b"][:, :], sb["qpre"][:, :],
                                start=True, stop=True, skip_group_check=True), inc="c_t")
                p.wait("v", "c_t")
                p.wait("v", "dtab")
                cos_off = 0 if kind == "q" else 1024
                sin_off = 512 if kind == "q" else 1536
                dst = sb["q_all"] if kind == "q" else sb["k_all"]
                p.do("v", lambda eng, sems, pq=pq, cos_off=cos_off:
                     eng.tensor_tensor(out=sb["rsc1"][:, :], in0=pq[:, :],
                                       in1=sb["tabs"][:, cos_off:cos_off + 512], op=OP.mult))
                p.do("v", lambda eng, sems, sin_off=sin_off:
                     eng.tensor_tensor(out=sb["rsc2"][:, :], in0=prot[:, :],
                                       in1=sb["tabs"][:, sin_off:sin_off + 512], op=OP.mult))
                vdrain()
                p.do("v", lambda eng, sems, dst=dst, h=h, tok0=tok0:
                     eng.tensor_tensor(out=dst[:, S * h + tok0:S * h + tok0 + 512],
                                       in0=sb["rsc1"][:, :], in1=sb["rsc2"][:, :], op=OP.add),
                     inc="c_v")
            else:
                for tt in range(4):
                    p.wait("t", "c_s", qpre_done)
                    p.wait("t", "c_v", ptv_free[0])
                    p.do("t", lambda eng, sems, tt=tt:
                         eng.transpose(ptv[:, 0:128], sb["qpre"][:, 128 * tt:128 * (tt + 1)],
                                       sb["ident_b"][:, :]), inc="c_t")
                    p.wait("v", "c_t")
                    p.do("v", lambda eng, sems, h=h, tok0=tok0, tt=tt:
                         eng.tensor_copy(out=sb["vt_all"][:, S * h + tok0 + 128 * tt:
                                                          S * h + tok0 + 128 * (tt + 1)],
                                         in_=ptv[:, 0:128]), inc="c_v")
                    ptv_free[0] = p.val("c_v")
            bank_free[of % 2] = p.val("c_v")
        panel_done_cv[pan] = p.val("c_v")
        tab_done_cv[pan] = p.val("c_v")
    p.barrier()
    st_psum2.close()
    if "q" in dmp:
        dma("y", "dout", dmp["q"], sb["q_all"][:, :])
        dma("y", "dout", dmp["k"], sb["k_all"][:, :])
        dma("y", "dout", dmp["vt"], sb["vt_all"][:, :])
        p.barrier()
    st_p2.close()

    alloc(st_p3, "e_pan", [128, 2 * S], BF16)
    alloc(st_p3, "zero_b", [128, 512], BF16)
    alloc(st_p3, "dslots", [128, 16], F32)
    alloc(st_p3, "dval", [128, 4], F32)
    alloc(st_p3, "invd_b", [128, 2], BF16)
    alloc(st_p3, "csrow", [128, 1024], F32)
    memset("g", sb["zero_b"][:, :], 0.0, inc="c_g")
    memset("v", sb["csrow"][:, :], 0.0)
    p.wait("t", "c_g")
    st_psum3 = ExitStack()
    ps0 = st_psum3.enter_context(nc.psum_tensor("ps0", [128, 512], F32))
    ps1 = st_psum3.enter_context(nc.psum_tensor("ps1", [128, 512], F32))
    ps2 = st_psum3.enter_context(nc.psum_tensor("ps2", [128, 512], F32))
    pcs0 = st_psum3.enter_context(nc.psum_tensor("pcs0", [128, 512], F32))
    pcs1 = st_psum3.enter_context(nc.psum_tensor("pcs1", [128, 512], F32))
    ptr = st_psum3.enter_context(nc.psum_tensor("ptr", [128, 512], F32))
    pbc = st_psum3.enter_context(nc.psum_tensor("pbc", [128, 512], F32))

    sq_pan = sb["e_pan"].ap()[:, 0:S]
    for h in range(NHC):
        for which in ("q", "k"):
            src = "q_all" if which == "q" else "k_all"
            p.wait("s", "c_t")
            p.do("s", lambda eng, sems, src=src, h=h:
                 eng.activation(out=sq_pan, in_=sb[src][:, S * h:S * (h + 1)],
                                func=AT.Square), inc="c_s")
            p.wait("t", "c_s")
            p.wait("t", "c_v")
            p.wait("t", "c_g")
            p.do("t", lambda eng, sems:
                 eng.matmul(pcs0[:, :], sb["ones_b"][:, 0:128], sb["zero_b"][:, :],
                            start=True, stop=True, skip_group_check=True))
            p.do("t", lambda eng, sems:
                 eng.matmul(pcs1[:, :], sb["ones_b"][:, 0:128], sb["zero_b"][:, :],
                            start=True, stop=True, skip_group_check=True))
            for kc in range(8):
                bank = (pcs0, pcs1)[kc // 4]
                j = kc % 4
                def mmn(eng, sems, bank=bank, j=j, kc=kc):
                    return eng.matmul(bank[32 * j:32 * j + 1, :],
                                      sb["ones_b"][:, 0:1],
                                      sq_pan[:, 512 * kc:512 * (kc + 1)],
                                      start=True, stop=True, skip_group_check=True,
                                      tile_position=(0, 32 * j))
                p.do("t", mmn, inc=("c_t" if kc == 7 else None))
            p.wait("v", "c_t")
            p.do("v", lambda eng, sems:
                 eng.reduce_max(sb["smax"][:, 0:1], pcs0[:, :], axis=AX))
            p.do("v", lambda eng, sems:
                 eng.reduce_max(sb["smax"][:, 1:2], pcs1[:, :], axis=AX))
            vdrain()
            p.do("v", lambda eng, sems:
                 eng.tensor_tensor(out=sb["smax"][:, 2:3], in0=sb["smax"][:, 0:1],
                                   in1=sb["smax"][:, 1:2], op=OP.max))
            vdrain()
            col = 32 * h + (0 if which == "q" else 1)
            p.do("v", lambda eng, sems, col=col:
                 eng.tensor_copy(out=sb["scr2"][:, col:col + 1], in_=sb["smax"][:, 2:3]),
                 inc="c_v")
    p.wait("t", "c_v")
    p.wait("t", "dwy")
    p.do("t", lambda eng, sems:
         eng.transpose(ptr[:, 0:128], sb["scr2"][:, :], sb["ident_f"][:, :]), inc="c_t")
    p.wait("v", "c_t")
    p.do("v", lambda eng, sems:
         eng.reduce_max(sb["smax"][:, 4:5], ptr[:, 0:128], axis=AX))
    vdrain()
    p.do("v", lambda eng, sems:
         eng.stream_shuffle(out=sb["smax"][:, 5:6], in_=sb["smax"][:, 4:5],
                            mask=[(i + 1) % 32 for i in range(32)]))
    vdrain()
    p.do("v", lambda eng, sems:
         eng.tensor_tensor(out=sb["smax"][:, 6:7], in0=sb["smax"][:, 4:5],
                           in1=sb["smax"][:, 5:6], op=OP.mult), inc="c_v")
    p.wait("s", "c_v")
    p.do("s", lambda eng, sems:
         eng.activation(out=sb["smax"][:, 7:8], in_=sb["smax"][:, 6:7], func=AT.Sqrt),
         inc="c_s")
    for h in range(NHC):
        p.wait("t", "c_s")
        def bcast(eng, sems, h=h):
            return eng.matmul(pbc[:, h:h + 1], sb["ones_f"][32 * h:32 * h + 1, :],
                              sb["smax"][32 * h:32 * h + 1, 7:8],
                              start=True, stop=True, skip_group_check=True,
                              tile_position=(32 * h, 0))
        p.do("t", bcast, inc=("c_t" if h == NHC - 1 else None))
    p.wait("v", "c_t")
    p.do("v", lambda eng, sems:
         eng.tensor_scalar(out=sb["negB4"][:, :], in0=pbc[:, 0:4],
                           scalar1=-1.0, scalar2=None, op0=OP.mult), inc="c_v")
    if "B4" in dmp:
        p.barrier()
        dma("y", "dout", dmp["B4"][:, 0:4], sb["negB4"][:, :])
        dma("y", "dout", dmp["B4"][:, 4:8], sb["smax"][:, 4:8])
    p.barrier()

    for h in range(NHC):
        s_bank_free = {0: 0, 1: 0, 2: 0}
        epan_free_ct = [0, 0]
        invd_free_ct = [0, 0]
        dslots_free_cv = [0, 0]
        pend = None
        p.wait("t", "c_v")
        for kc in range(8):
            bank = (pcs0, pcs1)[kc // 4]
            j = kc % 4
            def zmm(eng, sems, bank=bank, j=j):
                return eng.matmul(bank[32 * j:32 * j + 1, :], sb["ones_b"][:, 0:1],
                                  sb["zero_b"][:, :], start=True, stop=False,
                                  skip_group_check=True, tile_position=(0, 32 * j))
            p.do("t", zmm)
        def emit_colsum(qt, nkc, par):
            for kc in range(nkc):
                width = min(512, 128 * (qt + 1) - 512 * kc)
                bank = (pcs0, pcs1)[kc // 4]
                j = kc % 4
                stop = (qt == NQT - 1)
                def cmm(eng, sems, bank=bank, j=j, kc=kc, width=width, par=par, stop=stop, qt=qt):
                    epan = sb["e_pan"].ap()[:, par * S:par * S + S]
                    return eng.matmul(bank[32 * j:32 * j + 1, 0:width],
                                      sb["invd_b"][:, (qt % 2):(qt % 2) + 1],
                                      epan[:, 512 * kc:512 * kc + width],
                                      start=False, stop=stop, skip_group_check=True,
                                      tile_position=(0, 32 * j))
                p.do("t", cmm, inc=("c_t" if kc == nkc - 1 else None))
            epan_free_ct[par] = p.val("c_t")
            invd_free_ct[qt % 2] = p.val("c_t")
        for qt in range(NQT):
            nkc = (qt + 4) // 4
            par = qt % 2
            epan = sb["e_pan"].ap()[:, par * S:par * S + S]
            for kc in range(nkc):
                width = min(512, 128 * (qt + 1) - 512 * kc)
                bank = (ps0, ps1, ps2)[kc % 3]
                p.wait("t", "c_s", s_bank_free[kc % 3])
                def smm(eng, sems, bank=bank, h=h, qt=qt, kc=kc, width=width):
                    return eng.matmul(bank[:, 0:width],
                                      sb["q_all"][:, S * h + 128 * qt:S * h + 128 * (qt + 1)],
                                      sb["k_all"][:, S * h + 512 * kc:S * h + 512 * kc + width],
                                      start=True, stop=True, skip_group_check=True)
                p.do("t", smm, inc="c_t")
                mmv = p.val("c_t")
                is_diag = (kc == nkc - 1)
                if kc == 0:
                    p.wait("s", "c_t", epan_free_ct[par])
                    p.wait("s", "c_v", dslots_free_cv[par])
                if is_diag:
                    p.wait("v", "c_t", mmv)
                    p.wait("v", "dwy")
                    p.do("v", lambda eng, sems, bank=bank, width=width:
                         eng.tensor_tensor(out=bank[:, width - 128:width],
                                           in0=bank[:, width - 128:width],
                                           in1=sb["tri_add"][:, :], op=OP.add), inc="c_v")
                    p.wait("s", "c_v")
                else:
                    p.wait("s", "c_t", mmv)
                def expf(eng, sems, bank=bank, epan=epan, kc=kc, width=width, h=h, par=par):
                    return eng.activation(out=epan[:, 512 * kc:512 * kc + width],
                                          in_=bank[:, 0:width], func=AT.Exp,
                                          bias=sb["negB4"][:, h:h + 1],
                                          accum_out=sb["dslots"][:, 8 * par + kc:8 * par + kc + 1])
                p.do("s", expf, inc="c_s")
                s_bank_free[kc % 3] = p.val("c_s")
            if pend is not None:
                emit_colsum(*pend)
            p.wait("v", "c_s")
            p.wait("v", "c_t", invd_free_ct[par])
            p.do("v", lambda eng, sems, nkc=nkc, par=par:
                 eng.reduce_sum(sb["dval"][:, par:par + 1],
                                sb["dslots"][:, 8 * par:8 * par + nkc], axis=AX))
            vdrain()
            p.do("v", lambda eng, sems, par=par:
                 eng.reciprocal(out=sb["dval"][:, 2 + par:3 + par],
                                in_=sb["dval"][:, par:par + 1]))
            vdrain()
            p.do("v", lambda eng, sems, par=par:
                 eng.tensor_copy(out=sb["invd_b"][:, par:par + 1],
                                 in_=sb["dval"][:, 2 + par:3 + par]), inc="c_v")
            dslots_free_cv[par] = p.val("c_v")
            p.wait("t", "c_v")
            pend = (qt, nkc, par)
        emit_colsum(*pend)
        p.wait("v", "c_t")
        for kc in range(8):
            bank = (pcs0, pcs1)[kc // 4]
            j = kc % 4
            p.do("v", lambda eng, sems, bank=bank, j=j, kc=kc:
                 eng.tensor_copy(out=sb["csrow"][32 * j:32 * j + 1,
                                                 512 * (kc // 4):512 * (kc // 4) + 512],
                                 in_=bank[32 * j:32 * j + 1, :]),
                 inc=("c_v" if kc == 7 else None))
        for kt in range(NKT):
            kc = kt // 4
            j = kc % 4
            colband = 512 * (kc // 4) + 128 * (kt % 4)
            p.wait("t", "c_v")
            p.do("t", lambda eng, sems, colband=colband:
                 eng.transpose(ptr[:, 0:128], sb["csrow"][:, colband:colband + 128],
                               sb["ident_f"][:, :]), inc="c_t")
            p.wait("v", "c_t")
            p.do("v", lambda eng, sems, h=h, kt=kt, j=j:
                 eng.tensor_copy(out=sb["cs_bias"][:, 32 * h + kt:32 * h + kt + 1],
                                 in_=ptr[:, 32 * j:32 * j + 1]), inc="c_v")
        p.wait("t", "c_v")
    if "cs" in dmp:
        p.barrier()
        dma("y", "dout", dmp["cs"], sb["cs_bias"][:, :])
    p.barrier()

    BV = sb["binv"]
    p.do("v", lambda eng, sems: eng.memset(BV[:, 0:1], 0.0))
    for h in range(NHC):
        p.do("v", lambda eng, sems, h=h:
             eng.reduce_max(sb["smax"][:, 8:9], sb["cs_bias"][:, 32 * h:32 * h + 32], axis=AX))
        vdrain()
        p.do("v", lambda eng, sems, h=h:
             eng.tensor_copy(out=sb["scr2"][:, 32 * h + 2:32 * h + 3], in_=sb["smax"][:, 8:9]),
             inc="c_v")
    p.wait("t", "c_v")
    p.do("t", lambda eng, sems:
         eng.transpose(ptr[:, 0:128], sb["scr2"][:, :], sb["ident_f"][:, :]), inc="c_t")
    p.wait("v", "c_t")
    p.do("v", lambda eng, sems:
         eng.reduce_max(sb["smax"][:, 9:10], ptr[:, 0:128], axis=AX))
    vdrain()
    p.do("v", lambda eng, sems:
         eng.stream_shuffle(out=sb["smax"][:, 10:11], in_=sb["smax"][:, 9:10],
                            mask=[(i + 2) % 32 for i in range(32)]))
    vdrain()
    p.do("v", lambda eng, sems:
         eng.tensor_scalar(out=BV[:, 1:2], in0=sb["smax"][:, 10:11], scalar1=1.0001,
                           scalar2=1e-12, op0=OP.mult, op1=OP.add))
    vdrain()
    p.do("v", lambda eng, sems:
         eng.tensor_tensor(out=BV[:, 2:3], in0=BV[:, 0:1], in1=BV[:, 1:2], op=OP.add))
    vdrain()
    p.do("v", lambda eng, sems:
         eng.tensor_scalar(out=BV[:, 2:3], in0=BV[:, 2:3], scalar1=0.5,
                           scalar2=None, op0=OP.mult), inc="c_v")
    for it in range(BIS_ITERS):
        p.wait("t", "c_v")
        for h in range(NHC):
            def bc(eng, sems, h=h):
                return eng.matmul(pbc[:, h:h + 1], sb["ones_f"][32 * h:32 * h + 1, :],
                                  BV[32 * h:32 * h + 1, 2:3],
                                  start=True, stop=True, skip_group_check=True,
                                  tile_position=(32 * h, 0))
            p.do("t", bc, inc=("c_t" if h == NHC - 1 else None))
        p.wait("v", "c_t")
        p.do("v", lambda eng, sems:
             eng.tensor_copy(out=sb["thrb"][:, :], in_=pbc[:, 0:4]))
        vdrain()
        for h in range(NHC):
            p.do("v", lambda eng, sems, h=h:
                 eng.tensor_scalar(out=sb["heavy01"][:, 32 * h:32 * h + 32],
                                   in0=sb["cs_bias"][:, 32 * h:32 * h + 32],
                                   scalar1=sb["thrb"][:, h:h + 1], scalar2=None,
                                   op0=OP.is_ge))
        for h in range(NHC):
            p.do("v", lambda eng, sems, h=h:
                 eng.reduce_sum(sb["scr2"][:, 32 * h:32 * h + 1],
                                sb["heavy01"][:, 32 * h:32 * h + 32], axis=AX),
                 inc=("c_v" if h == NHC - 1 else None))
        p.wait("t", "c_v")
        p.do("t", lambda eng, sems:
             eng.transpose(ptr[:, 0:128], sb["scr2"][:, :], sb["ident_f"][:, :]), inc="c_t")
        p.wait("v", "c_t")
        p.do("v", lambda eng, sems:
             eng.reduce_sum(BV[:, 4:5], ptr[:, 0:128], axis=AX))
        vdrain()
        p.do("v", lambda eng, sems:
             eng.tensor_scalar(out=BV[:, 3:4], in0=BV[:, 4:5], scalar1=408.5,
                               scalar2=None, op0=OP.is_ge))
        p.do("v", lambda eng, sems:
             eng.tensor_tensor(out=BV[:, 5:6], in0=BV[:, 2:3], in1=BV[:, 0:1], op=OP.subtract))
        p.do("v", lambda eng, sems:
             eng.tensor_tensor(out=BV[:, 6:7], in0=BV[:, 2:3], in1=BV[:, 1:2], op=OP.subtract))
        p.do("v", lambda eng, sems:
             eng.tensor_scalar(out=BV[:, 7:8], in0=BV[:, 3:4], scalar1=-1.0,
                               scalar2=1.0, op0=OP.mult, op1=OP.add))
        p.do("v", lambda eng, sems:
             eng.tensor_tensor(out=BV[:, 8:9], in0=BV[:, 5:6], in1=BV[:, 3:4], op=OP.mult))
        p.do("v", lambda eng, sems:
             eng.tensor_tensor(out=BV[:, 9:10], in0=BV[:, 6:7], in1=BV[:, 7:8], op=OP.mult))
        vdrain()
        p.do("v", lambda eng, sems:
             eng.tensor_tensor(out=BV[:, 0:1], in0=BV[:, 0:1], in1=BV[:, 8:9], op=OP.add))
        p.do("v", lambda eng, sems:
             eng.tensor_tensor(out=BV[:, 1:2], in0=BV[:, 1:2], in1=BV[:, 9:10], op=OP.add))
        vdrain()
        p.do("v", lambda eng, sems:
             eng.tensor_tensor(out=BV[:, 2:3], in0=BV[:, 0:1], in1=BV[:, 1:2], op=OP.add))
        vdrain()
        p.do("v", lambda eng, sems:
             eng.tensor_scalar(out=BV[:, 2:3], in0=BV[:, 2:3], scalar1=0.5,
                               scalar2=None, op0=OP.mult), inc="c_v")
    p.wait("t", "c_v")
    for h in range(NHC):
        def bc2(eng, sems, h=h):
            return eng.matmul(pbc[:, h:h + 1], sb["ones_f"][32 * h:32 * h + 1, :],
                              BV[32 * h:32 * h + 1, 0:1],
                              start=True, stop=True, skip_group_check=True,
                              tile_position=(32 * h, 0))
        p.do("t", bc2, inc=("c_t" if h == NHC - 1 else None))
    p.wait("v", "c_t")
    p.do("v", lambda eng, sems:
         eng.tensor_copy(out=sb["thrb"][:, :], in_=pbc[:, 0:4]))
    for h in range(NHC):
        p.do("v", lambda eng, sems, h=h:
             eng.tensor_scalar(out=sb["smax"][:, 12 + h:13 + h], in0=sb["negB4"][:, h:h + 1],
                               scalar1=-1e9, scalar2=None, op0=OP.add))
    for h in range(NHC):
        p.do("v", lambda eng, sems, h=h:
             eng.tensor_scalar(out=sb["heavy01"][:, 32 * h:32 * h + 32],
                               in0=sb["cs_bias"][:, 32 * h:32 * h + 32],
                               scalar1=sb["thrb"][:, h:h + 1], scalar2=None,
                               op0=OP.is_ge))
    for h in range(NHC):
        p.do("v", lambda eng, sems, h=h:
             eng.tensor_scalar(out=sb["hbias"][:, 32 * h:32 * h + 32],
                               in0=sb["heavy01"][:, 32 * h:32 * h + 32],
                               scalar1=1.0, scalar2=1e9,
                               op0=OP.subtract, op1=OP.mult))
    vdrain()
    for h in range(NHC):
        p.do("v", lambda eng, sems, h=h:
             eng.tensor_scalar(out=sb["hbias"][:, 32 * h:32 * h + 32],
                               in0=sb["hbias"][:, 32 * h:32 * h + 32],
                               scalar1=sb["negB4"][:, h:h + 1], scalar2=None,
                               op0=OP.add),
             inc=("c_v" if h == NHC - 1 else None))
    if "thr" in dmp:
        p.barrier()
        dma("y", "dout", dmp["thr"][:, 0:4], sb["thrb"][:, :])
        dma("y", "dout", dmp["thr"][:, 4:8], BV[:, 0:4])
    if "heavy" in dmp:
        p.barrier()
        dma("y", "dout", dmp["heavy"], sb["heavy01"][:, :])
    if "hb" in dmp:
        p.barrier()
        dma("y", "dout", dmp["hb"], sb["hbias"][:, :])
    p.barrier()
    st_psum3.close()

    alloc(st_p5, "attn_all", [128, NHC * S], BF16)
    alloc(st_p5, "etile", [128, 2 * 512], BF16)
    alloc(st_p5, "bcsb", [128, 512], F32)
    alloc(st_p5, "rdp", [1, 512], F32)
    st_psum5 = ExitStack()
    pst0 = st_psum5.enter_context(nc.psum_tensor("pst0", [128, 512], F32))
    pst1 = st_psum5.enter_context(nc.psum_tensor("pst1", [128, 512], F32))
    pout = st_psum5.enter_context(nc.psum_tensor("pout", [128, 512], F32))
    pd = st_psum5.enter_context(nc.psum_tensor("pd", [128, 512], F32))
    pbc5 = st_psum5.enter_context(nc.psum_tensor("pbc5", [128, 512], F32))

    st_free_s = {0: 0, 1: 0}
    et_free_t = {0: 0, 1: 0}
    pv_guard_cv = [0]
    p.wait("v", "dw")
    p.wait("v", "dwy")
    p.wait("s", "dw")
    p.wait("s", "dwy")
    for h in range(NHC):
        for qp in range(8):
            q0 = 512 * qp
            nkt = 4 * qp + 4
            mmv = {}

            def emit_S(kt, h=h, qp=qp, q0=q0, nkt=nkt):
                j0 = max(0, kt - 4 * qp)
                live0 = 128 * j0
                width = 512 - live0
                bank = (pst0, pst1)[kt % 2]
                p.wait("t", "c_s", st_free_s[kt % 2])
                if kt == 0:
                    p.wait("t", "c_v", pv_guard_cv[0])
                def stmm(eng, sems, bank=bank, h=h, kt=kt, q0=q0, live0=live0, width=width):
                    return eng.matmul(bank[:, live0:live0 + width],
                                      sb["k_all"][:, S * h + 128 * kt:S * h + 128 * (kt + 1)],
                                      sb["q_all"][:, S * h + q0 + live0:S * h + q0 + 512],
                                      start=True, stop=True, skip_group_check=True)
                p.do("t", stmm, inc="c_t")
                mmv[kt] = p.val("c_t")

            def emit_exp_masks(kt, h=h, qp=qp, nkt=nkt):
                j0 = max(0, kt - 4 * qp)
                live0 = 128 * j0
                width = 512 - live0
                bank = (pst0, pst1)[kt % 2]
                bsel = kt % 2
                subs = []
                for j in range(4):
                    if j < j0:
                        continue
                    D = 128 * (4 * qp + j - kt)
                    subs.append((j, D))
                all_far = all(D >= 640 for _, D in subs)
                esl = sb["etile"].ap()[:, 512 * (kt % 2):512 * (kt % 2) + 512]
                p.wait("s", "c_t", max(mmv[kt], et_free_t[kt % 2]))
                need_v = False
                if all_far:
                    def expb(eng, sems, bank=bank, esl=esl, h=h, kt=kt, live0=live0, width=width):
                        return eng.activation(out=esl[:, live0:live0 + width],
                                              in_=bank[:, live0:live0 + width], func=AT.Exp,
                                              bias=sb["hbias"][:, 32 * h + kt:32 * h + kt + 1])
                    p.do("s", expb, inc="c_s")
                    st_free_s[bsel] = p.val("c_s")
                else:
                    far_j = [j for j, D in subs if D >= 640]
                    near_j = [j for j, D in subs if D < 640]
                    nj0, nj1 = near_j[0], near_j[-1]
                    def expn(eng, sems, bank=bank, esl=esl, h=h, a=128 * nj0, b=128 * (nj1 + 1)):
                        return eng.activation(out=esl[:, a:b], in_=bank[:, a:b], func=AT.Exp,
                                              bias=sb["negB4"][:, h:h + 1])
                    p.do("s", expn, inc=("c_s" if not far_j else None))
                    if far_j:
                        fa, fb = 128 * far_j[0], 128 * (far_j[-1] + 1)
                        def expf2(eng, sems, bank=bank, esl=esl, h=h, kt=kt, a=fa, b=fb):
                            return eng.activation(out=esl[:, a:b], in_=bank[:, a:b], func=AT.Exp,
                                                  bias=sb["hbias"][:, 32 * h + kt:32 * h + kt + 1])
                        p.do("s", expf2, inc="c_s")
                    st_free_s[bsel] = p.val("c_s")
                    mops = []
                    for j, D in subs:
                        aa = 128 * j
                        if D == 0:
                            mops.append(lambda eng, sems, esl=esl, aa=aa:
                                        eng.tensor_tensor(out=esl[:, aa:aa + 128],
                                                          in0=esl[:, aa:aa + 128],
                                                          in1=sb["tri01_b"][:, :], op=OP.mult))
                        elif D in (384, 512):
                            mask = sb["band384_b"] if D == 384 else sb["band512_b"]
                            mops.append(lambda eng, sems, esl=esl, aa=aa, mask=mask, h=h, kt=kt:
                                        eng.scalar_tensor_tensor(
                                            out=esl[:, aa:aa + 128], in0=mask[:, :],
                                            scalar=sb["heavy01"][:, 32 * h + kt:32 * h + kt + 1],
                                            in1=esl[:, aa:aa + 128], op0=OP.max, op1=OP.mult))
                    need_v = bool(mops)
                    if need_v:
                        p.wait("v", "c_s")
                        for mi, mop in enumerate(mops):
                            p.do("v", mop, inc=("c_v" if mi == len(mops) - 1 else None))
                return need_v

            def emit_dpv(kt, need_v, h=h, qp=qp, nkt=nkt):
                j0 = max(0, kt - 4 * qp)
                live0 = 128 * j0
                width = 512 - live0
                esl = sb["etile"].ap()[:, 512 * (kt % 2):512 * (kt % 2) + 512]
                if need_v:
                    p.wait("t", "c_v")
                else:
                    p.wait("t", "c_s")
                def dmm(eng, sems, esl=esl, kt=kt, nkt=nkt, live0=live0, width=width):
                    return eng.matmul(pd[0:1, live0:live0 + width], sb["ones_b"][:, 0:1],
                                      esl[:, live0:live0 + width],
                                      start=(kt == 0), stop=(kt == nkt - 1),
                                      skip_group_check=True)
                p.do("t", dmm)
                def pvmm(eng, sems, esl=esl, h=h, kt=kt, nkt=nkt, live0=live0, width=width):
                    return eng.matmul(pout[:, live0:live0 + width],
                                      sb["vt_all"][:, S * h + 128 * kt:S * h + 128 * (kt + 1)],
                                      esl[:, live0:live0 + width],
                                      start=(kt == 0), stop=(kt == nkt - 1),
                                      skip_group_check=True)
                p.do("t", pvmm, inc="c_t")
                et_free_t[kt % 2] = p.val("c_t")

            emit_S(0)
            for kt in range(nkt):
                if kt + 1 < nkt:
                    emit_S(kt + 1)
                nv = emit_exp_masks(kt)
                emit_dpv(kt, nv)
            p.wait("v", "c_t")
            p.do("v", lambda eng, sems:
                 eng.reciprocal(out=sb["rdp"][0:1, :], in_=pd[0:1, :]), inc="c_v")
            p.wait("t", "c_v")
            p.do("t", lambda eng, sems:
                 eng.matmul(pbc5[:, :], sb["ones_f"][0:1, :], sb["rdp"][0:1, :],
                            start=True, stop=True, skip_group_check=True), inc="c_t")
            p.wait("s", "c_t")
            p.do("s", lambda eng, sems:
                 eng.activation(out=sb["bcsb"][:, :], in_=pbc5[:, :], func=AT.Copy),
                 inc="c_s")
            p.wait("v", "c_s")
            p.do("v", lambda eng, sems, h=h, q0=q0:
                 eng.tensor_tensor(out=sb["attn_all"][:, S * h + q0:S * h + q0 + 512],
                                   in0=pout[:, :], in1=sb["bcsb"][:, :], op=OP.mult),
                 inc="c_v")
            pv_guard_cv[0] = p.val("c_v")
    if "attn" in dmp:
        p.barrier()
        dma("y", "dout", dmp["attn"], sb["attn_all"][:, :])
    p.barrier()
    st_psum5.close()

    for h in range(NHC):
        dma("y", "dcc", cc_in.ap()[128 * h:128 * (h + 1), :],
            sb["attn_all"][:, S * h:S * (h + 1)])
    p.wait("g", "dcc")
    def ccall(eng, sems):
        return eng.collective_compute(
            "AllGather", OP.bypass,
            replica_groups=[[0, 1, 2, 3], [4, 5, 6, 7]],
            ins=[cc_in.ap().opt()],
            outs=[cc_out.ap().opt()],
        )
    p.do("g", ccall, inc="cc_done")
    p.barrier()
    st_p5.close()
    st_p3.close()
    st_qkv.close()

    alloc(st_p6a, "h2full", [128, 2 * 32 * 512], BF16)
    st_p6r = ExitStack()
    alloc(st_p6r, "xbf0", [128, DM], F32)
    alloc(st_p6r, "xbf1", [128, DM], F32)
    alloc(st_p6r, "sqd6", [128, DM], BF16)
    alloc(st_p6r, "xno0", [128, DM], BF16)
    alloc(st_p6r, "xno1", [128, DM], BF16)
    st_psum6 = ExitStack()
    ph0 = st_psum6.enter_context(nc.psum_tensor("ph0", [128, 512], F32))
    ph1 = st_psum6.enter_context(nc.psum_tensor("ph1", [128, 512], F32))

    for e_ in ("v", "s", "g", "t", "y"):
        p.wait(e_, "dcc")

    ln_tiles(xch_in, TCH // 128, xnch_scr, "dxf", "sqd6", (sb["xno0"], sb["xno1"]),
             (sb["xbf0"], sb["xbf1"]))
    p.barrier()

    st_p6r.close()
    alloc(st_p6b, "xn_ffn", [128, 32 * 512], BF16)
    alloc(st_p6b, "h1full", [128, 2 * 32 * 512], BF16)
    alloc(st_p6b, "w1c0", [128, 16 * 128], BF16)
    alloc(st_p6b, "w1c1", [128, 16 * 128], BF16)
    alloc(st_p6b, "w2c0", [128, 32 * 128], BF16)
    alloc(st_p6b, "w2c1", [128, 32 * 128], BF16)
    ffn_bank_free = {0: 0, 1: 0}
    w1_used_ct = [0, 0]
    w2_used_ct = [0, 0]
    p.wait("y", "dxf")
    for th in range(2):
        for fc in range(16):
            dma("y", "dxr", sb["xn_ffn"][:, 1024 * fc + 512 * th:1024 * fc + 512 * (th + 1)],
                xnch_scr.ap()[512 * th:512 * (th + 1), 128 * fc:128 * (fc + 1)], transpose=True)
    if "xnf" in dmp:
        p.wait("y", "dxr")
        dma("y", "dout", dmp["xnf"], sb["xn_ffn"][:, 0:16 * 512])
    for hf in range(32):
        wbuf = (sb["w1c0"], sb["w1c1"])[hf % 2]
        p.wait("g", "c_t", w1_used_ct[hf % 2])
        dma("g", "dwf", wbuf[:, :], w1_in[128 * hf:128 * (hf + 1), :])
        p.wait("t", "dwf")
        p.wait("t", "dxr")
        for tp in range(2):
            bank = (ph0, ph1)[tp]
            p.wait("t", "c_s", ffn_bank_free[tp])
            for fc in range(16):
                def hmm(eng, sems, bank=bank, wbuf=wbuf, fc=fc, tp=tp,
                        first=(fc == 0), last=(fc == 15)):
                    return eng.matmul(bank[:, :], wbuf[:, 128 * fc:128 * (fc + 1)],
                                      sb["xn_ffn"][:, 1024 * fc + 512 * tp:1024 * fc + 512 * (tp + 1)],
                                      start=first, stop=last, skip_group_check=True)
                p.do("t", hmm, inc=("c_t" if fc == 15 else None))
            p.wait("s", "c_t")
            p.do("s", lambda eng, sems, bank=bank, hf=hf, tp=tp:
                 eng.activation(out=sb["h1full"][:, 1024 * hf + 512 * tp:1024 * hf + 512 * (tp + 1)],
                                in_=bank[:, :],
                                func=AT.Relu, bias=sb["b1_sb"][:, hf:hf + 1]), inc="c_s")
            ffn_bank_free[tp] = p.val("c_s")
        w1_used_ct[hf % 2] = p.val("c_t")
    if "h1" in dmp:
        p.barrier()
        dma("y", "dout", dmp["h1"], sb["h1full"][:, 0:16 * 1024])
    p.barrier()
    for hf2 in range(32):
        wbuf = (sb["w2c0"], sb["w2c1"])[hf2 % 2]
        p.wait("g", "c_t", w2_used_ct[hf2 % 2])
        dma("g", "dwf", wbuf[:, :], w2_in[128 * hf2:128 * (hf2 + 1), :])
        p.wait("t", "dwf")
        for tp in range(2):
            bank = (ph0, ph1)[tp]
            p.wait("t", "c_s", ffn_bank_free[tp])
            for fc in range(32):
                def h2mm(eng, sems, bank=bank, wbuf=wbuf, fc=fc, tp=tp,
                         first=(fc == 0), last=(fc == 31)):
                    return eng.matmul(bank[:, :], wbuf[:, 128 * fc:128 * (fc + 1)],
                                      sb["h1full"][:, 1024 * fc + 512 * tp:1024 * fc + 512 * (tp + 1)],
                                      start=first, stop=last, skip_group_check=True)
                p.do("t", h2mm, inc=("c_t" if fc == 31 else None))
            p.wait("s", "c_t")
            p.do("s", lambda eng, sems, bank=bank, hf2=hf2, tp=tp:
                 eng.activation(out=sb["h2full"][:, 1024 * hf2 + 512 * tp:1024 * hf2 + 512 * (tp + 1)],
                                in_=bank[:, :],
                                func=AT.Relu, bias=sb["b2_sb"][:, hf2:hf2 + 1]), inc="c_s")
            ffn_bank_free[tp] = p.val("c_s")
        w2_used_ct[hf2 % 2] = p.val("c_t")
    if "h2" in dmp:
        p.wait("y", "c_s")
        dma("y", "dout", dmp["h2"], sb["h2full"][:, :])
    p.barrier()
    st_psum6.close()
    st_p6b.close()

    alloc(st_p6b, "agat", [128, 32 * 512], BF16)
    alloc(st_p6b, "gtmp0", [128, S], BF16)
    alloc(st_p6b, "gtmp1", [128, S], BF16)
    alloc(st_p6b, "woc0", [128, 16 * 256], BF16)
    alloc(st_p6b, "woc1", [128, 16 * 256], BF16)
    alloc(st_p6b, "w3c0", [128, 32 * 256], BF16)
    alloc(st_p6b, "w3c1", [128, 32 * 256], BF16)
    alloc(st_p6b, "b3_sb", [1, DM], BF16)
    alloc(st_p6b, "xsl0", [128, 256], F32)
    alloc(st_p6b, "xsl1", [128, 256], F32)
    alloc(st_p6b, "osl0", [128, 256], F32)
    alloc(st_p6b, "osl1", [128, 256], F32)
    st_psum7 = ExitStack()
    pa0 = st_psum7.enter_context(nc.psum_tensor("pa0", [128, 512], F32))
    pa1 = st_psum7.enter_context(nc.psum_tensor("pa1", [128, 512], F32))

    dma("g", "dwf", sb["b3_sb"][:, :], b3_in)
    p.wait("y", "cc_done", 1)
    gsel_done_cv = [0, 0]
    for afp in range(8):
        af0, af1 = 2 * afp, 2 * afp + 1
        p.wait("y", "c_v", gsel_done_cv[0])
        dma("y", "dag", sb["gtmp0"][:, :], cc_out.ap()[128 * af0:128 * (af0 + 1), :])
        p.wait("y", "c_v", gsel_done_cv[1])
        dma("y", "dag", sb["gtmp1"][:, :], cc_out.ap()[128 * af1:128 * (af1 + 1), :])
        p.wait("v", "dag")
        for cch in range(4):
            if cch > 0:
                vdrain()
            for gi, (gb, af) in enumerate(((sb["gtmp0"], af0), (sb["gtmp1"], af1))):
                if cch == 0:
                    p.do("v", lambda eng, sems, gb=gb, af=af:
                         eng.tensor_scalar(out=sb["agat"][:, 1024 * af:1024 * (af + 1)],
                                           in0=gb[:, 0:TCH], scalar1=sb["selm"][:, 0:1],
                                           scalar2=None, op0=OP.mult))
                else:
                    p.do("v", lambda eng, sems, gb=gb, af=af, cch=cch:
                         eng.scalar_tensor_tensor(out=sb["agat"][:, 1024 * af:1024 * (af + 1)],
                                                  in0=gb[:, TCH * cch:TCH * (cch + 1)],
                                                  scalar=sb["selm"][:, cch:cch + 1],
                                                  in1=sb["agat"][:, 1024 * af:1024 * (af + 1)],
                                                  op0=OP.mult, op1=OP.add),
                         inc=("c_v" if (cch == 3 and gi == 1) else None))
        gsel_done_cv[0] = p.val("c_v")
        gsel_done_cv[1] = p.val("c_v")
    p.barrier()
    if "agat" in dmp:
        p.wait("y", "c_v")
        dma("y", "dout", dmp["agat"], sb["agat"][:, :])
    as_bank_free = {0: 0, 1: 0}
    wo_used_ct = [0, 0]
    x_used_cv = [0, 0]
    out_dma_done = [0, 0]
    for oc in range(8):
        wob = (sb["woc0"], sb["woc1"])[oc % 2]
        w3b = (sb["w3c0"], sb["w3c1"])[oc % 2]
        p.wait("g", "c_t", wo_used_ct[oc % 2])
        dma("g", "dwf", wob[:, :], wo_in[128 * oc:128 * (oc + 1), :])
        dma("g", "dwf", w3b[:, :], w3_in[128 * oc:128 * (oc + 1), :])
        p.wait("t", "dwf")
        p.wait("t", "dag")
        for tt in range(8):
            xb = (sb["xsl0"], sb["xsl1"])[tt % 2]
            ob = (sb["osl0"], sb["osl1"])[tt % 2]
            p.wait("y", "c_v", x_used_cv[tt % 2])
            dma("y", "dx", xb[:, :],
                xch_in[128 * tt:128 * (tt + 1), 256 * oc:256 * (oc + 1)])
            bank = (pa0, pa1)[tt % 2]
            p.wait("t", "c_v", as_bank_free[tt % 2])
            def b3mm(eng, sems, bank=bank, oc=oc):
                return eng.matmul(bank[:, 0:256], sb["ones_b"][0:1, 0:128],
                                  sb["b3_sb"][0:1, 256 * oc:256 * (oc + 1)],
                                  start=True, stop=False, skip_group_check=True)
            p.do("t", b3mm)
            for fc in range(16):
                def womm(eng, sems, bank=bank, wob=wob, fc=fc, tt=tt):
                    return eng.matmul(bank[:, 0:256],
                                      sb["agat"][:, 1024 * fc + 128 * tt:1024 * fc + 128 * (tt + 1)],
                                      wob[:, 256 * fc:256 * (fc + 1)],
                                      start=False, stop=False, skip_group_check=True)
                p.do("t", womm)
            for fc in range(32):
                def w3mm(eng, sems, bank=bank, w3b=w3b, fc=fc, tt=tt, last=(fc == 31)):
                    return eng.matmul(bank[:, 0:256],
                                      sb["h2full"][:, 1024 * fc + 512 * (tt // 4) + 128 * (tt % 4):
                                                   1024 * fc + 512 * (tt // 4) + 128 * (tt % 4 + 1)],
                                      w3b[:, 256 * fc:256 * (fc + 1)],
                                      start=False, stop=last, skip_group_check=True)
                p.do("t", w3mm, inc=("c_t" if fc == 31 else None))
            p.wait("v", "c_t")
            p.wait("v", "dx")
            p.wait("v", "dout", out_dma_done[tt % 2])
            p.do("v", lambda eng, sems, bank=bank, xb=xb, ob=ob:
                 eng.tensor_tensor(out=ob[:, :], in0=bank[:, 0:256],
                                   in1=xb[:, :], op=OP.add), inc="c_v")
            as_bank_free[tt % 2] = p.val("c_v")
            x_used_cv[tt % 2] = p.val("c_v")
            p.wait("y", "c_v")
            dma("y", "dout", out_c[128 * tt:128 * (tt + 1), 256 * oc:256 * (oc + 1)],
                ob[:, :])
            out_dma_done[tt % 2] = p.val("dout")
        wo_used_ct[oc % 2] = p.val("c_t")
    p.wait("y", "dout")
    p.barrier()
    st_psum7.close()
    st_p6b.close()
    st_p6a.close()
    st_perm.close()

    with ExitStack() as bctx:
        sems = {}
        for s_ in p.sem_names:
            sems[s_] = bctx.enter_context(nc.semaphore(s_))
        block = bctx.enter_context(nc.Block())

        @block.tensor
        def _(eng):
            p.replay("t", eng, sems)

        @block.vector
        def _(eng):
            p.replay("v", eng, sems)

        @block.scalar
        def _(eng):
            p.replay("s", eng, sems)

        @block.gpsimd
        def _(eng):
            p.replay("g", eng, sems)

        @block.sync
        def _(eng):
            p.replay("y", eng, sems)

    return nc, dmp


def _make_inputs(full):
    x = np.ascontiguousarray(full["L0_attn_hidden_states"])
    cst = _host_constants()
    maps = []
    for core in range(8):
        b, hg = core // 4, core % 4
        m = {
            "x_b": np.ascontiguousarray(x[b]),
            "x_chunk": np.ascontiguousarray(x[b, TCH * hg:TCH * (hg + 1), :]),
            "wq_t": np.ascontiguousarray(full["L0_attn_Wq"][512 * hg:512 * (hg + 1), :].T),
            "wk_t": np.ascontiguousarray(full["L0_attn_Wk"][512 * hg:512 * (hg + 1), :].T),
            "wv_t": np.ascontiguousarray(full["L0_attn_Wv"][512 * hg:512 * (hg + 1), :].T),
            "wo_tl": np.ascontiguousarray(
                full["L0_attn_Wo"].reshape(8, 256, 16, 128).transpose(0, 3, 2, 1)
                .reshape(1024, 4096)),
            "w1_tl": np.ascontiguousarray(
                full["L0_ffn_W1"].reshape(32, 128, 16, 128).transpose(0, 3, 2, 1)
                .reshape(FF, DM)),
            "w2_tl": np.ascontiguousarray(
                full["L0_ffn_W2"].reshape(32, 128, 32, 128).transpose(0, 3, 2, 1)
                .reshape(FF, FF)),
            "w3_tl": np.ascontiguousarray(
                full["L0_ffn_W3"].reshape(8, 256, 32, 128).transpose(0, 3, 2, 1)
                .reshape(1024, 8192)),
            "b1s": np.ascontiguousarray(full["L0_ffn_b1"].reshape(32, 128).T),
            "b2s": np.ascontiguousarray(full["L0_ffn_b2"].reshape(32, 128).T),
            "b3r": np.ascontiguousarray(full["L0_ffn_b3"].reshape(1, DM)),
            "selmask": np.ascontiguousarray(
                np.repeat((np.arange(4) == hg).astype(np.float32)[None, :], 128, 0)),
        }
        for k, v in cst.items():
            m["c_" + k] = v
        maps.append(m)
    return maps


def kernel(**inputs):
    nc, _ = build_program(dump=_DUMP)
    maps = _make_inputs(inputs)
    res = run_bass_kernel_spmd(nc, maps, core_ids=list(range(8)))
    out = np.empty((2, S, DM), dtype=np.float32)
    for core in range(8):
        b, hg = core // 4, core % 4
        out[b, TCH * hg:TCH * (hg + 1), :] = res.results[core]["out_c"]
    return out
